# version 02
# baseline (speedup 1.0000x reference)
import numpy as np
import ml_dtypes
import concourse.bass as bass
import concourse.mybir as mybir
from concourse.bass_utils import run_bass_kernel_spmd

F32 = mybir.dt.float32
BF16 = mybir.dt.bfloat16
I32 = mybir.dt.int32
AF = mybir.ActivationFunctionType
ALU = mybir.AluOpType
AX = mybir.AxisListType

NCORES = 8
BC = 2
S = 2048
D = 1024
DC = 8
TT = 512
TPB = S // TT
E = 8
F_SH = 2048
FC = F_SH // 128

EPS_LN = 1e-5
EPS_RMS = 1e-12
EPS_COS = 1e-8
SQRT_D = float(np.sqrt(D))
BSTOK = BC * S


class Prog:
    ENGINES = ("sync", "tensor", "vector", "scalar", "gpsimd")
    DMA_ENG = ("sync", "gpsimd")

    NDMA = 10

    def __init__(self, nc):
        self.nc = nc
        self.ops = []
        self.count = {e: 0 for e in self.ENGINES}
        self.dma_idx = {e: 0 for e in self.DMA_ENG}
        self.dma_uses = {}
        self.last_w = {}
        self.readers = {}

    def op(self, engine, fn, reads=(), writes=(), extra=(), cc=False,
           gp_compute=False):
        waits = set()
        for k in reads:
            if k is None:
                continue
            d = self.last_w.get(k)
            if d:
                waits.add(d)
        for k in writes:
            d = self.last_w.get(k)
            if d:
                waits.add(d)
            for d2 in self.readers.get(k, ()):
                waits.add(d2)
        for d in extra:
            if d:
                waits.add(d)
        if cc:
            sname = "cc_%d" % self.dma_idx.setdefault("cc", 0)
            self.dma_idx["cc"] = self.dma_idx["cc"] + 1
            self.dma_uses[sname] = 1
            dep = (sname, 1)
            amt = 1
            incname = sname
        elif engine in self.DMA_ENG and not gp_compute:
            i = self.dma_idx[engine] % self.NDMA
            self.dma_idx[engine] += 1
            sname = f"d_{engine}_{i}"
            prior = self.dma_uses.get(sname, 0)
            if prior:
                waits.add((sname, 16 * prior))
            self.dma_uses[sname] = prior + 1
            dep = (sname, 16 * (prior + 1))
            amt = 16
            incname = sname
        else:
            self.count[engine] += 1
            dep = (engine, self.count[engine])
            amt = 1
            incname = engine
        w2 = []
        for (e2, v2) in waits:
            if e2 == incname and v2 < dep[1]:
                if engine not in self.DMA_ENG and e2 == engine and False:
                    pass
            if e2 == engine and engine not in self.DMA_ENG and v2 >= dep[1]:
                continue
            if e2 == "tensor" and engine == "tensor":
                continue
            w2.append((e2, v2))
        self.ops.append((engine, fn, w2, amt, incname))
        for k in writes:
            self.last_w[k] = dep
            self.readers[k] = []
        for k in reads:
            if k is not None:
                self.readers.setdefault(k, []).append(dep)
        return dep

    def build(self):
        nc = self.nc
        import contextlib
        sem_names = list(self.ENGINES) + sorted(self.dma_uses)
        with contextlib.ExitStack() as stack:
            sems = {n: stack.enter_context(nc.semaphore("q_" + str(n)))
                    for n in sem_names}
            with nc.Block() as block:
                for eng_name in self.ENGINES:
                    ops = [(fn, w, amt, incn) for (e, fn, w, amt, incn) in self.ops
                           if e == eng_name]
                    if not ops:
                        continue

                    def body(eng, ops=ops, eng_name=eng_name):
                        hi = {}
                        for fn, waits, amt, incn in ops:
                            for (e2, v2) in waits:
                                if hi.get(e2, -1) >= v2:
                                    continue
                                hi[e2] = v2
                                eng.wait_ge(sems[e2], v2)
                            inst = fn(eng)
                            assert inst is not None, f"op on {eng_name} returned None"
                            inst.then_inc(sems[incn], amt)
                    getattr(block, eng_name)(body)


def ts(i, n=128):
    return slice(i * n, (i + 1) * n)


def host_tiles(wt, kc, ec):
    K, Ez = wt.shape
    assert K == kc * 128 and Ez == ec * 128, (wt.shape, kc, ec)
    t = wt.reshape(kc, 128, ec, 128).transpose(1, 2, 0, 3).reshape(128, ec * kc * 128)
    return np.ascontiguousarray(t)


def build_pre(debug=False):
    nc = bass.Bass("TRN2", target_bir_lowering=False, num_devices=NCORES)

    x_in = nc.dram_tensor("x", [BC, DC, 128, S], F32, kind="ExternalInput")
    wqkva = nc.dram_tensor("wqkva", [128, 24 * 8 * 128], F32, kind="ExternalInput")
    gamma = nc.dram_tensor("gamma", [128, DC], F32, kind="ExternalInput")
    w_ih = nc.dram_tensor("w_ih", [128, 24 * 8 * 128], F32, kind="ExternalInput")
    ctx_w = nc.dram_tensor("ctx_w", [128, 8 * 8 * 128], F32, kind="ExternalInput")
    ctx_b = nc.dram_tensor("ctx_b", [128, DC], F32, kind="ExternalInput")
    expr_w = nc.dram_tensor("expr_w", [128, 8 * 8 * 128], F32, kind="ExternalInput")
    sh_g = nc.dram_tensor("sh_g", [128, 16 * 8 * 128], F32, kind="ExternalInput")
    sh_u = nc.dram_tensor("sh_u", [128, 16 * 8 * 128], F32, kind="ExternalInput")
    sh_d = nc.dram_tensor("sh_d", [128, 8 * 16 * 128], F32, kind="ExternalInput")

    xn_o = nc.dram_tensor("xn_o", [BC, DC, 128, S], BF16, kind="ExternalOutput")
    smix_o = nc.dram_tensor("smix_o", [BC, DC, 128, S], F32, kind="ExternalOutput")
    xw_o = nc.dram_tensor("xw_o", [24, BC, 128, S], BF16, kind="ExternalOutput")
    expr_o = nc.dram_tensor("expr_o", [BC, DC, 128, S], BF16, kind="ExternalOutput")
    sh_o = nc.dram_tensor("sh_o", [BC, DC, 128, S], F32, kind="ExternalOutput")
    h0_o = nc.dram_tensor("h0_o", [128, DC * BC], F32, kind="ExternalOutput")
    if debug:
        dbg_w = nc.dram_tensor("dbg_w", [128, 4096], F32, kind="ExternalOutput")
        dbg_xn2 = nc.dram_tensor("dbg_xn2", [128, DC * TT], F32, kind="ExternalOutput")
        dbg_a = nc.dram_tensor("dbg_a", [128, TT], F32, kind="ExternalOutput")

    p = Prog(nc)

    import contextlib
    with contextlib.ExitStack() as _st:
        wA = _st.enter_context(nc.sbuf_tensor("wA", [128, 24576], BF16))
        wB = _st.enter_context(nc.sbuf_tensor("wB", [128, 8192], BF16))
        wD = _st.enter_context(nc.sbuf_tensor("wD", [128, 16384], BF16))
        gsc = _st.enter_context(nc.sbuf_tensor("gsc", [128, DC], F32))
        cbs = _st.enter_context(nc.sbuf_tensor("cbs", [128, DC], F32))
        ones_col = _st.enter_context(nc.sbuf_tensor("ones_col", [1, 128], F32))
        c32_col = _st.enter_context(nc.sbuf_tensor("c32_col", [1, 128], F32))
        ones_vec = _st.enter_context(nc.sbuf_tensor("ones_vec", [128, 1], F32))
        ceps = _st.enter_context(nc.sbuf_tensor("ceps", [128, 1], F32))
        xf = _st.enter_context(nc.sbuf_tensor("xf", [128, DC * TT], F32))
        xsq = _st.enter_context(nc.sbuf_tensor("xsq", [128, TT], F32))
        rowA = _st.enter_context(nc.sbuf_tensor("rowA", [1, 4 * TT], F32))
        xnt = _st.enter_context(nc.sbuf_tensor("xnt", [128, DC * TT], BF16))
        xn2_fm = _st.enter_context(nc.sbuf_tensor("xn2_fm", [128, DC * TT], BF16))
        tmpf = _st.enter_context(nc.sbuf_tensor("tmpf", [128, TT], F32))
        asg = _st.enter_context(nc.sbuf_tensor("asg", [128, TT], F32))
        kvs = _st.enter_context(nc.sbuf_tensor("kvs", [128, TT], F32))
        carry = _st.enter_context(nc.sbuf_tensor("carry", [128, DC], F32))
        ev1 = _st.enter_context(nc.sbuf_tensor("ev1", [128, 2 * TT], F32))
        hid = _st.enter_context(nc.sbuf_tensor("hid", [128, FC * TT], BF16))
        macc = _st.enter_context(nc.sbuf_tensor("macc", [128, DC * BC], F32))
        mred = _st.enter_context(nc.sbuf_tensor("mred", [128, DC], F32))
        mbf = _st.enter_context(nc.sbuf_tensor("mbf", [128, DC * BC], BF16))
        h0s = _st.enter_context(nc.sbuf_tensor("h0s", [128, DC * BC], F32))
        psA = _st.enter_context(nc.psum_tensor("psA", [128, 512], F32))
        psB = _st.enter_context(nc.psum_tensor("psB", [128, 512], F32))
        psR1 = _st.enter_context(nc.psum_tensor("psR1", [1, 512], F32))
        psR2 = _st.enter_context(nc.psum_tensor("psR2", [1, 512], F32))
        psC = _st.enter_context(nc.psum_tensor("psC", [128, 512], F32))
        psD = _st.enter_context(nc.psum_tensor("psD", [128, 512], F32))
        p.op("vector", lambda e: e.memset(ones_col[:], 1.0), writes=["ones"])
        p.op("vector", lambda e: e.memset(c32_col[:], SQRT_D), writes=["c32"])
        p.op("vector", lambda e: e.memset(ones_vec[:], 1.0), writes=["onesv"])
        p.op("vector", lambda e: e.memset(ceps[:], EPS_LN), writes=["ceps"])
        p.op("vector", lambda e: e.memset(macc[:], 0.0), writes=["macc"])
        p.op("vector", lambda e: e.memset(carry[:], 0.0), writes=["carry"])
        p.op("sync", lambda e: e.dma_start(out=gsc[:], in_=gamma[:, :]), writes=["gsc"])
        p.op("sync", lambda e: e.dma_start(out=cbs[:], in_=ctx_b[:, :]), writes=["cbs"])
        p.op("gpsimd", lambda e: e.dma_start(out=wA[:], in_=wqkva[:, :]), writes=["wA"])
        if debug:
            for j in range(8):
                p.op("vector", (lambda e, j=j: e.tensor_copy(
                    ev1[:, 0:TT], wA[:, j * TT:(j + 1) * TT])),
                    reads=["wA"], writes=[("ev", 0)])
                p.op("gpsimd", (lambda e, j=j: e.dma_start(
                    out=dbg_w[:, j * TT:(j + 1) * TT], in_=ev1[:, 0:TT])),
                    reads=[("ev", 0)])

        def xn_cols(k, tile=None):
            return xnt[:, k * TT:(k + 1) * TT]

        def load_xnt(b, t):
            p.op("sync", (lambda e, b=b, t=t: e.dma_start(
                out=xnt[:].rearrange("p (c f) -> p c f", c=DC),
                in_=xn_o.ap()[b, :, :, t * TT:(t + 1) * TT].rearrange(
                    "c p f -> p c f"))),
                writes=[("xnb", k) for k in range(DC)])

        def acc_mms(psum, wbuf, ec, act_cols, kc, wkey, akeys, pskey, pextra=()):
            dep = None
            for k in range(kc):
                dep = p.op(
                    "tensor",
                    (lambda e, k=k: e.matmul(
                        psum, wbuf[:, (ec * kc + k) * 128:(ec * kc + k + 1) * 128],
                        act_cols(k), start=(k == 0), stop=(k == kc - 1))),
                    reads=[wkey, akeys[k] if isinstance(akeys, list) else akeys],
                    writes=[pskey], extra=pextra if k == 0 else ())
            return dep

        dram_xn_dep = {}
        for b in range(BC):
            for t in range(TPB):
                tile = b * TPB + t
                p.op("sync", (lambda e, b=b, t=t: e.dma_start(
                    out=xf[:].rearrange("p (c f) -> p c f", c=DC),
                    in_=x_in.ap()[b, :, :, t * TT:(t + 1) * TT].rearrange(
                        "c p f -> p c f"))),
                    writes=["xf"])

                for k in range(DC):
                    p.op("tensor", (lambda e, k=k: e.matmul(
                        psR1[:, 0:TT], ones_vec[:, :], xf[:, ts(k, TT)],
                        start=(k == 0), stop=(k == DC - 1))),
                        reads=["xf", "onesv"], writes=["psR1"])
                p.op("scalar", lambda e: e.activation(
                    rowA[:, 0:TT], psR1[:, 0:TT], AF.Copy, scale=-1.0 / D),
                    reads=["psR1"], writes=["row0"])
                for k in range(DC):
                    p.op("scalar", (lambda e, k=k: e.activation(
                        xsq[:], xf[:, ts(k, TT)], AF.Square)),
                        reads=["xf"], writes=["xsq"])
                    p.op("tensor", (lambda e, k=k: e.matmul(
                        psR2[:, 0:TT], ones_vec[:, :], xsq[:],
                        start=(k == 0), stop=(k == DC - 1))),
                        reads=["xsq", "onesv"], writes=["psR2"])
                p.op("scalar", lambda e: e.activation(
                    rowA[:, TT:2 * TT], rowA[:, 0:TT], AF.Square),
                    reads=["row0"], writes=["row1"])
                p.op("vector", lambda e: e.scalar_tensor_tensor(
                    rowA[:, 2 * TT:3 * TT], psR2[:, 0:TT], 1.0 / D,
                    rowA[:, TT:2 * TT], ALU.mult, ALU.subtract),
                    reads=["psR2", "row1"], writes=["row2"])
                p.op("scalar", lambda e: e.activation(
                    rowA[:, 3 * TT:4 * TT], rowA[:, 2 * TT:3 * TT], AF.Ln, bias=ceps[0:1, 0:1]),
                    reads=["row2", "ceps"], writes=["row3"])
                p.op("scalar", lambda e: e.activation(
                    rowA[:, 2 * TT:3 * TT], rowA[:, 3 * TT:4 * TT], AF.Exp, scale=-0.5),
                    reads=["row3"], writes=["row2"])
                p.op("tensor", lambda e: e.matmul(
                    psC[:, 0:TT], ones_col[:, :], rowA[:, 0:TT], start=True, stop=True),
                    reads=["row0", "ones"], writes=["psC"])
                p.op("tensor", lambda e: e.matmul(
                    psD[:, 0:TT], ones_col[:, :], rowA[:, 2 * TT:3 * TT],
                    start=True, stop=True),
                    reads=["row2", "ones"], writes=["psD"])
                for k in range(DC):
                    p.op("vector", (lambda e, k=k: e.tensor_add(
                        tmpf[:], xf[:, ts(k, TT)], psC[:, 0:TT])),
                        reads=["xf", "psC"], writes=["tmpf"])
                    p.op("vector", (lambda e, k=k, xc=xn_cols(k, tile): e.tensor_mul(
                        xc, tmpf[:], psD[:, 0:TT])),
                        reads=["tmpf", "psD"], writes=[("xnb", k)])

                d_xn_store = p.op("gpsimd", (lambda e, b=b, t=t: e.dma_start(
                    out=xn_o.ap()[b, :, :, t * TT:(t + 1) * TT].rearrange(
                        "c p n -> p c n"),
                    in_=xnt[:].rearrange("p (c n) -> p c n", c=DC))),
                    reads=[("xnb", k) for k in range(DC)])
                dram_xn_dep[tile] = d_xn_store

                for k in range(DC):
                    p.op("vector", (lambda e, k=k, xc=xn_cols(k, tile): e.tensor_reduce(
                        mred[:, k:k + 1], xc, AX.X, ALU.add)),
                        reads=[("xnb", k)], writes=[("mred", k)])
                    p.op("vector", (lambda e, k=k, b=b: e.tensor_add(
                        macc[:, k * BC + b:k * BC + b + 1],
                        macc[:, k * BC + b:k * BC + b + 1],
                        mred[:, k:k + 1])),
                        reads=[("mred", k)], writes=["macc"])

                for k in range(DC):
                    p.op("scalar", (lambda e, k=k, xc=xn_cols(k, tile): e.activation(
                        xsq[:], xc, AF.Square)),
                        reads=[("xnb", k)], writes=["xsq"])
                    p.op("tensor", (lambda e, k=k: e.matmul(
                        psR2[:, 0:TT], ones_vec[:, :], xsq[:],
                        start=(k == 0), stop=(k == DC - 1))),
                        reads=["xsq", "onesv"], writes=["psR2"])
                p.op("scalar", lambda e: e.activation(
                    rowA[:, 0:TT], psR2[:, 0:TT], AF.Sqrt),
                    reads=["psR2"], writes=["row0"])
                p.op("vector", lambda e: e.tensor_scalar_max(
                    rowA[:, TT:2 * TT], rowA[:, 0:TT], EPS_RMS),
                    reads=["row0"], writes=["row1"])
                p.op("vector", lambda e: e.reciprocal(
                    rowA[:, 2 * TT:3 * TT], rowA[:, TT:2 * TT]),
                    reads=["row1"], writes=["row2"])
                p.op("tensor", lambda e: e.matmul(
                    psC[:, 0:TT], c32_col[:, :], rowA[:, 2 * TT:3 * TT],
                    start=True, stop=True),
                    reads=["row2", "c32"], writes=["psC"])
                for k in range(DC):
                    p.op("vector", (lambda e, k=k, xc=xn_cols(k, tile): e.scalar_tensor_tensor(
                        xn2_fm[:, ts(k, TT)], xc, gsc[:, k:k + 1], psC[:, 0:TT],
                        ALU.mult, ALU.mult)),
                        reads=[("xnb", k), "gsc", "psC"], writes=[("xn2", k)])

                if debug and tile == 0:
                    for k in range(DC):
                        p.op("vector", (lambda e, k=k: e.tensor_copy(
                            ev1[:, 0:TT], xn2_fm[:, ts(k, TT)])),
                            reads=[("xn2", k)], writes=[("ev", 0)])
                        p.op("gpsimd", (lambda e, k=k: e.dma_start(
                            out=dbg_xn2[:, k * TT:(k + 1) * TT], in_=ev1[:, 0:TT])),
                            reads=[("ev", 0)])
                xn2keys = [("xn2", k) for k in range(DC)]
                for c in range(DC):
                    acc_mms(psA[:, 0:TT], wA, 16 + c,
                            lambda k: xn2_fm[:, ts(k, TT)], 8, "wA", xn2keys, "psA")
                    p.op("scalar", lambda e: e.activation(
                        asg[:], psA[:, 0:TT], AF.Sigmoid),
                        reads=["psA"], writes=["asg"])
                    if debug and tile == 0 and c == 0:
                        p.op("gpsimd", (lambda e: e.dma_start(
                            out=dbg_a[:, :], in_=asg[:])),
                            reads=["asg"])
                    acc_mms(psB[:, 0:TT], wA, 8 + c,
                            lambda k: xn2_fm[:, ts(k, TT)], 8, "wA", xn2keys, "psB")
                    p.op("vector", (lambda e, c=c, t=t: e.tensor_tensor_scan(
                        kvs[:], asg[:], psB[:, 0:TT],
                        (0.0 if t == 0 else carry[:, c:c + 1]),
                        ALU.mult, ALU.add)),
                        reads=["asg", "psB", "carry"], writes=["kvs"])
                    p.op("vector", (lambda e, c=c: e.tensor_copy(
                        carry[:, c:c + 1], kvs[:, TT - 1:TT])),
                        reads=["kvs"], writes=["carry"])
                    acc_mms(psA[:, 0:TT], wA, c,
                            lambda k: xn2_fm[:, ts(k, TT)], 8, "wA", xn2keys, "psA")
                    eh = ev1[:, (c % 2) * TT:(c % 2) * TT + TT]
                    p.op("vector", (lambda e, eh=eh: e.tensor_mul(
                        eh, psA[:, 0:TT], kvs[:])),
                        reads=["psA", "kvs"], writes=[("ev", c % 2)])
                    p.op("gpsimd", (lambda e, b=b, t=t, c=c, eh=eh: e.dma_start(
                        out=smix_o.ap()[b, c, :, t * TT:(t + 1) * TT], in_=eh)),
                        reads=[("ev", c % 2)])

        p.op("gpsimd", lambda e: e.dma_start(out=wA[:], in_=w_ih[:, :]), writes=["wA"])
        for b in range(BC):
            for t in range(TPB):
                tile = b * TPB + t
                load_xnt(b, t)
                xkeys = [("xnb", k) for k in range(DC)]
                for c in range(24):
                    acc_mms(psA[:, 0:TT], wA, c,
                            (lambda k, tile=tile: xn_cols(k, tile)),
                            8, "wA", xkeys, "psA")
                    eh = ev1[:, (c % 2) * TT:(c % 2) * TT + TT]
                    p.op("scalar", (lambda e, eh=eh: e.activation(
                        eh, psA[:, 0:TT], AF.Copy)),
                        reads=["psA"], writes=[("ev", c % 2)])
                    p.op("gpsimd", (lambda e, b=b, t=t, c=c, eh=eh: e.dma_start(
                        out=xw_o.ap()[c, b, :, t * TT:(t + 1) * TT], in_=eh)),
                        reads=[("ev", c % 2)])

        p.op("gpsimd", lambda e: e.dma_start(out=wB[:], in_=expr_w[:, :]), writes=["wB"])
        for b in range(BC):
            for t in range(TPB):
                tile = b * TPB + t
                load_xnt(b, t)
                xkeys = [("xnb", k) for k in range(DC)]
                for c in range(DC):
                    acc_mms(psA[:, 0:TT], wB, c,
                            (lambda k, tile=tile: xn_cols(k, tile)),
                            8, "wB", xkeys, "psA")
                    eh = ev1[:, (c % 2) * TT:(c % 2) * TT + TT]
                    p.op("scalar", (lambda e, eh=eh: e.activation(
                        eh, psA[:, 0:TT], AF.Copy)),
                        reads=["psA"], writes=[("ev", c % 2)])
                    p.op("gpsimd", (lambda e, b=b, t=t, c=c, eh=eh: e.dma_start(
                        out=expr_o.ap()[b, c, :, t * TT:(t + 1) * TT], in_=eh)),
                        reads=[("ev", c % 2)])
        p.op("gpsimd", lambda e: e.dma_start(out=wB[:], in_=ctx_w[:, :]), writes=["wB"])
        p.op("vector", lambda e: e.tensor_scalar_mul(mbf[:], macc[:], 1.0 / S),
             reads=["macc"], writes=["mbf"])
        for c in range(DC):
            for k in range(DC):
                p.op("tensor", (lambda e, c=c, k=k: e.matmul(
                    psA[:, c * BC:(c + 1) * BC],
                    wB[:, (c * DC + k) * 128:(c * DC + k + 1) * 128],
                    mbf[:, k * BC:(k + 1) * BC],
                    start=(k == 0), stop=(k == DC - 1))),
                    reads=["wB", "mbf"], writes=["psA"])
        for c in range(DC):
            p.op("vector", (lambda e, c=c: e.tensor_scalar_add(
                h0s[:, c * BC:(c + 1) * BC], psA[:, c * BC:(c + 1) * BC],
                cbs[:, c:c + 1])),
                reads=["psA", "cbs"], writes=["h0s"])
        p.op("gpsimd", lambda e: e.dma_start(out=h0_o[:, :], in_=h0s[:, :]),
             reads=["h0s"])

        p.op("gpsimd", lambda e: e.dma_start(out=wA[:, 0:16384], in_=sh_g[:, :]),
             writes=["wA"])
        p.op("gpsimd", lambda e: e.dma_start(out=wA[:, 16384:24576], in_=sh_u[:, 0:8192]),
             writes=["wA"])
        p.op("gpsimd", lambda e: e.dma_start(out=wB[:, :], in_=sh_u[:, 8192:16384]),
             writes=["wB"])
        p.op("gpsimd", lambda e: e.dma_start(out=wD[:, :], in_=sh_d[:, :]),
             writes=["wD"])
        for b in range(BC):
            for t in range(TPB):
                tile = b * TPB + t
                load_xnt(b, t)
                xkeys = [("xnb", k) for k in range(DC)]
                for f in range(FC):
                    acc_mms(psA[:, 0:TT], wA, f,
                            (lambda k, tile=tile: xn_cols(k, tile)),
                            8, "wA", xkeys, "psA")
                    p.op("scalar", lambda e: e.activation(
                        ev1[:, 0:TT], psA[:, 0:TT], AF.Silu),
                        reads=["psA"], writes=[("ev", 0)])
                    if f < 8:
                        acc_mms(psB[:, 0:TT], wA, 16 + f,
                                (lambda k, tile=tile: xn_cols(k, tile)),
                                8, "wA", xkeys, "psB")
                    else:
                        acc_mms(psB[:, 0:TT], wB, f - 8,
                                (lambda k, tile=tile: xn_cols(k, tile)),
                                8, "wB", xkeys, "psB")
                    p.op("vector", (lambda e, f=f: e.tensor_mul(
                        hid[:, f * TT:(f + 1) * TT], ev1[:, 0:TT], psB[:, 0:TT])),
                        reads=[("ev", 0), "psB"], writes=[("hid", f)])
                hkeys = [("hid", f) for f in range(FC)]
                for c in range(DC):
                    for k in range(FC):
                        p.op("tensor", (lambda e, c=c, k=k: e.matmul(
                            psA[:, 0:TT],
                            wD[:, (c * FC + k) * 128:(c * FC + k + 1) * 128],
                            hid[:, k * TT:(k + 1) * TT],
                            start=(k == 0), stop=(k == FC - 1))),
                            reads=["wD", ("hid", k)], writes=["psA"])
                    eh = ev1[:, (c % 2) * TT:(c % 2) * TT + TT]
                    p.op("scalar", (lambda e, eh=eh: e.activation(
                        eh, psA[:, 0:TT], AF.Copy)),
                        reads=["psA"], writes=[("ev", c % 2)])
                    p.op("gpsimd", (lambda e, b=b, t=t, c=c, eh=eh: e.dma_start(
                        out=sh_o.ap()[b, c, :, t * TT:(t + 1) * TT], in_=eh)),
                        reads=[("ev", c % 2)])

        p.build()
    return nc


def pre_in_maps(inputs):
    x = inputs["L0_attn_x"]
    wq_t = host_tiles(np.ascontiguousarray(inputs["L0_attn_W_qkva"].T), 8, 24)
    gam = np.ascontiguousarray(inputs["L0_attn_gamma"].reshape(DC, 128).T)
    wih_t = host_tiles(np.ascontiguousarray(inputs["L0_ffn_gru_w_ih"].T), 8, 24)
    ctx_t = host_tiles(np.ascontiguousarray(inputs["L0_ffn_ctx_w"].T), 8, 8)
    ctxb = np.ascontiguousarray(inputs["L0_ffn_ctx_b"].reshape(DC, 128).T)
    expr_t = host_tiles(np.ascontiguousarray(inputs["L0_ffn_expr_w"].T), 8, 8)
    shg_t = host_tiles(np.ascontiguousarray(inputs["L0_ffn_shared_gate_w"].T), 8, 16)
    shu_t = host_tiles(np.ascontiguousarray(inputs["L0_ffn_shared_up_w"].T), 8, 16)
    shd_t = host_tiles(np.ascontiguousarray(inputs["L0_ffn_shared_down_w"].T), 16, 8)
    maps = []
    for c in range(NCORES):
        xs = x[c * BC:(c + 1) * BC]
        x_fm = np.ascontiguousarray(xs.transpose(0, 2, 1)).reshape(BC, DC, 128, S)
        maps.append({
            "x": x_fm, "wqkva": wq_t, "gamma": gam, "w_ih": wih_t,
            "ctx_w": ctx_t, "ctx_b": ctxb, "expr_w": expr_t,
            "sh_g": shg_t, "sh_u": shu_t, "sh_d": shd_t,
        })
    return maps


def _chunks8(v):
    return np.ascontiguousarray(v.reshape(8, 128).T)


def kernel(**inputs):
    from concourse.bass_utils import run_bass_kernel_spmd
    inputs = {k: np.asarray(v) for k, v in inputs.items()}
    cores = list(range(NCORES))

    pre_maps = pre_in_maps(inputs)
    rpre = run_bass_kernel_spmd(build_pre(), pre_maps, core_ids=cores).results

    whh_t = host_tiles(np.ascontiguousarray(inputs["L0_ffn_gru_w_hh"].T), 8, 24)
    nc_g = build_gru_chunk()
    h = [np.asarray(rpre[c]["h0_o"]) for c in range(NCORES)]
    hs_parts = [[] for _ in range(NCORES)]
    for ch in range(S // CH):
        gmaps = [{"whh_t": whh_t, "h_in": h[c],
                  "xw_i": np.ascontiguousarray(
                      rpre[c]["xw_o"][:, :, :, ch * CH:(ch + 1) * CH])}
                 for c in range(NCORES)]
        rg = run_bass_kernel_spmd(nc_g, gmaps, core_ids=cores).results
        for c in range(NCORES):
            h[c] = np.asarray(rg[c]["h_out"])
            hs_parts[c].append(np.asarray(rg[c]["hs_o"]))

    wg_s = np.stack([host_tiles(np.ascontiguousarray(
        inputs["L0_ffn_gate_w"][e].T), 8, 16) for e in range(E)])
    wu_s = np.stack([host_tiles(np.ascontiguousarray(
        inputs["L0_ffn_up_w"][e].T), 8, 16) for e in range(E)])
    wd_s = np.stack([host_tiles(np.ascontiguousarray(
        inputs["L0_ffn_down_w"][e].T), 16, 8) for e in range(E)])
    mmaps = []
    for c in range(NCORES):
        mmaps.append({
            "hs_i": np.concatenate(hs_parts[c], axis=1),
            "expr_i": np.asarray(rpre[c]["expr_o"]),
            "xn_i": np.asarray(rpre[c]["xn_o"]),
            "smix_i": np.asarray(rpre[c]["smix_o"]),
            "sh_i": np.asarray(rpre[c]["sh_o"]),
            "x_i": pre_maps[c]["x"],
            "wg_i": wg_s, "wu_i": wu_s, "wd_i": wd_s,
        })
    rm = run_bass_kernel_spmd(build_moe(), mmaps, core_ids=cores).results

    wv = inputs["L1_attn_wv"]
    wv_rows = np.ascontiguousarray(
        wv.reshape(8, 128, 1024).transpose(1, 0, 2)).reshape(128, 8192)
    bv = inputs["L1_attn_bv"]
    l1_common = {
        "wq_i": host_tiles(inputs["L1_attn_wq"], 8, 8),
        "wk_i": host_tiles(inputs["L1_attn_wk"], 8, 8),
        "wo_i": host_tiles(inputs["L1_attn_wo"], 8, 8),
        "wv_i": wv_rows,
        "bq_i": _chunks8(inputs["L1_attn_bq"]),
        "bk_i": _chunks8(inputs["L1_attn_bk"]),
        "bv_i": _chunks8(bv),
        "bvr_i": np.ascontiguousarray(bv.reshape(1, 1024)),
        "bo_i": _chunks8(inputs["L1_attn_bo"]),
        "wg_i": host_tiles(np.ascontiguousarray(inputs["L1_ffn_w_gate"].T), 8, 24),
        "wu_i": host_tiles(np.ascontiguousarray(inputs["L1_ffn_w_up"].T), 8, 24),
        "wd_i": host_tiles(np.ascontiguousarray(inputs["L1_ffn_w_down"].T), 24, 8),
    }
    lmaps = [dict(l1_common, x1_i=np.asarray(rm[c]["x1_o"])) for c in range(NCORES)]
    rl = run_bass_kernel_spmd(build_l1(), lmaps, core_ids=cores).results

    out = np.empty((NCORES * BC, S, 1024), np.float32)
    for c in range(NCORES):
        o = np.asarray(rl[c]["out_o"]).reshape(BC, 1024, S)
        out[c * BC:(c + 1) * BC] = o.transpose(0, 2, 1)
    return out


if __name__ == "__main__":
    nc = build_pre()
    print("built pre:", nc.n_instructions(), "instructions")


GWIN = 1024
HCOLS = DC * BC
PCOLS = 24 * BC


STATIC_XW = False
GRU_SKIP = set()
GRU_STRIP = set()
USE_NCFORI = False


def build_gru():
    nc = bass.Bass("TRN2", target_bir_lowering=False, num_devices=NCORES)

    whh_t = nc.dram_tensor("whh_t", [128, 24 * 8 * 128], F32, kind="ExternalInput")
    h0_i = nc.dram_tensor("h0_i", [128, HCOLS], F32, kind="ExternalInput")
    xw_i = nc.dram_tensor("xw_i", [24, BC, 128, S], BF16, kind="ExternalInput")
    hs_o = nc.dram_tensor("hs_o", [128, S * HCOLS], BF16, kind="ExternalOutput")

    NW = S // GWIN
    XWC = 24 * BC * GWIN
    RING = GWIN * HCOLS
    QCOL = RING // 4

    import contextlib
    with contextlib.ExitStack() as st:
        whh = st.enter_context(nc.sbuf_tensor("whh", [128, 24576], BF16))
        xwin = st.enter_context(nc.sbuf_tensor("xwin", [128, XWC], BF16))
        hb0 = st.enter_context(nc.sbuf_tensor("hb0", [128, HCOLS], BF16))
        hb1 = st.enter_context(nc.sbuf_tensor("hb1", [128, HCOLS], BF16))
        hf0 = st.enter_context(nc.sbuf_tensor("hf0", [128, HCOLS], F32))
        hf1 = st.enter_context(nc.sbuf_tensor("hf1", [128, HCOLS], F32))
        rz = st.enter_context(nc.sbuf_tensor("rz", [128, 2 * HCOLS], F32))
        rzs = st.enter_context(nc.sbuf_tensor("rzs", [128, 2 * HCOLS], F32))
        t0 = st.enter_context(nc.sbuf_tensor("t0", [128, HCOLS], F32))
        t0b = st.enter_context(nc.sbuf_tensor("t0b", [128, HCOLS], F32))
        nn = st.enter_context(nc.sbuf_tensor("nn", [128, HCOLS], F32))
        d1 = st.enter_context(nc.sbuf_tensor("d1", [128, HCOLS], F32))
        d2 = st.enter_context(nc.sbuf_tensor("d2", [128, HCOLS], F32))
        hstage = st.enter_context(nc.sbuf_tensor("hstage", [128, RING], BF16))
        ps0 = st.enter_context(nc.psum_tensor("ps0", [128, PCOLS], F32))
        ps1 = st.enter_context(nc.psum_tensor("ps1", [128, PCOLS], F32))

        sem_pe = st.enter_context(nc.semaphore("sem_pe"))
        sem_v = st.enter_context(nc.semaphore("sem_v"))
        sem_s = st.enter_context(nc.semaphore("sem_s"))
        sem_g = st.enter_context(nc.semaphore("sem_g"))
        s_init = st.enter_context(nc.semaphore("s_init"))
        s_init2 = st.enter_context(nc.semaphore("s_init2"))
        s_xwl = st.enter_context(nc.semaphore("s_xwl"))
        s_hsd = st.enter_context(nc.semaphore("s_hsd"))

        PE, V, A_, G, SP = nc.tensor, nc.vector, nc.scalar, nc.gpsimd, nc.sync
        hb = [hb0, hb1]
        hf = [hf0, hf1]
        ps = [ps0, ps1]

        loop_regs = nc.alloc_registers("gru_loop_i")
        _loop_n = [0]
        import contextlib as _cl

        @_cl.contextmanager
        def my_fori(n_iter):
            _loop_n[0] += 1
            lname = f"gruseg_{_loop_n[0]}"
            nc.regs_mov(loop_regs, 0)
            nc.br(lname + "_loop", engines=mybir.ALL_ENGINES)
            with nc.body(lname + "_loop", valid_engines=mybir.ALL_ENGINES):
                yield
                nc.regs_alu(loop_regs, loop_regs, 1, op=ALU.add)
                nc.br_lt(loop_regs, n_iter, on_true=lname + "_loop",
                         on_false=lname + "_end", engines=mybir.ALL_ENGINES)
            nc.switch_bb(lname + "_end")

        r_pe_g = PE.alloc_register("r_pe_g")
        rv = {n: V.alloc_register(n) for n in
              ("rp1", "rp2", "rs1", "rs2", "rgv", "vxw", "ghs", "rva", "rfw")}
        ra = {n: A_.alloc_register(n) for n in ("rv1", "rga")}
        rg = {n: G.alloc_register(n) for n in
              ("rv2", "rv3", "rv4", "rpe", "gxw")}

        if "init" not in GRU_STRIP:
            G.dma_start(out=whh[:], in_=whh_t[:, :]).then_inc(s_init, 16)
            G.dma_start(out=hf0[:], in_=h0_i[:, :]).then_inc(s_init, 16)
            V.wait_ge(s_init, 32)
            V.tensor_copy(hb0[:], hf0[:]).then_inc(s_init2, 1)

        xw_loads = 0

        def load_window(w):
            nonlocal xw_loads
            for c in range(24):
                for b in range(BC):
                    SP.dma_start(
                        out=xwin[:, (c * BC + b) * GWIN:(c * BC + b) * GWIN + GWIN],
                        in_=xw_i.ap()[c, b, :, w * GWIN:(w + 1) * GWIN],
                    ).then_inc(s_xwl, 16)
                    xw_loads += 1

        if "loads" not in GRU_STRIP:
            load_window(0)

        def emit_step(s):
            p = s % 2
            q = 1 - p
            PE.wait_ge(sem_g, r_pe_g)
            PE.reg_add(r_pe_g, r_pe_g, 3)
            if "pe" in GRU_SKIP:
                PE.nop().then_inc(sem_pe, 1)
                PE.nop().then_inc(sem_pe, 1)
                return emit_step_rest(s)
            inst = None
            for ej in range(16):
                for ki in range(8):
                    inst = PE.matmul(
                        ps[p][:, ej * BC:(ej + 1) * BC],
                        whh[:, (ej * 8 + ki) * 128:(ej * 8 + ki + 1) * 128],
                        hb[p][:, ki * BC:(ki + 1) * BC],
                        start=(ki == 0), stop=(ki == 7))
            inst.then_inc(sem_pe, 1)
            for ej in range(16, 24):
                for ki in range(8):
                    inst = PE.matmul(
                        ps[p][:, ej * BC:(ej + 1) * BC],
                        whh[:, (ej * 8 + ki) * 128:(ej * 8 + ki + 1) * 128],
                        hb[p][:, ki * BC:(ki + 1) * BC],
                        start=(ki == 0), stop=(ki == 7))
            inst.then_inc(sem_pe, 1)
            return emit_step_rest(s)

        def emit_step_rest(s):
            p = s % 2
            q = 1 - p
            V.wait_ge(sem_pe, rv["rp1"])
            V.reg_add(rv["rp1"], rv["rp1"], 2)
            if "dve" in GRU_SKIP:
                V.wait_ge(sem_s, rv["rs1"]); V.reg_add(rv["rs1"], rv["rs1"], 2)
                V.wait_ge(sem_pe, rv["rp2"]); V.reg_add(rv["rp2"], rv["rp2"], 2)
                V.wait_ge(sem_s, rv["rs2"]); V.reg_add(rv["rs2"], rv["rs2"], 2)
                V.wait_ge(sem_v, rv["rfw"]); V.reg_add(rv["rfw"], rv["rfw"], 5)
                V.wait_ge(sem_g, rv["rgv"]); V.reg_add(rv["rgv"], rv["rgv"], 3)
                V.wait_ge(sem_v, rv["rva"]); V.reg_add(rv["rva"], rv["rva"], 5)
                for _ in range(5):
                    V.engine_nop().then_inc(sem_v, 1)
                V.reg_add(rv["ghs"], rv["ghs"], HCOLS)
                V.reg_add(rv["vxw"], rv["vxw"], 1)
                return emit_act_pool(s)
            V.tensor_add(
                rz[:],
                ps[p][:, 0:2 * HCOLS],
                bass.AP(xwin, 0, [[XWC, 128], [BC * GWIN, 16], [GWIN, BC]]) if STATIC_XW else
                bass.AP(xwin, rv["vxw"], [[XWC, 128], [BC * GWIN, 16], [GWIN, BC]]),
            ).then_inc(sem_v, 1)
            V.wait_ge(sem_s, rv["rs1"])
            V.reg_add(rv["rs1"], rv["rs1"], 2)
            V.wait_ge(sem_pe, rv["rp2"])
            V.reg_add(rv["rp2"], rv["rp2"], 2)
            V.tensor_mul(t0[:], rzs[:, 0:HCOLS], ps[p][:, 2 * HCOLS:3 * HCOLS]
                         ).then_inc(sem_v, 1)
            V.wait_ge(sem_s, rv["rs2"])
            V.reg_add(rv["rs2"], rv["rs2"], 2)
            V.wait_ge(sem_v, rv["rfw"])
            V.reg_add(rv["rfw"], rv["rfw"], 5)
            V.tensor_sub(d1[:], hf[p][:], nn[:]).then_inc(sem_v, 1)
            V.wait_ge(sem_g, rv["rgv"])
            V.reg_add(rv["rgv"], rv["rgv"], 3)
            V.tensor_add(hf[q][:], nn[:], d2[:]).then_inc(sem_v, 1)
            V.wait_ge(sem_v, rv["rva"])
            V.reg_add(rv["rva"], rv["rva"], 5)
            V.tensor_copy(
                bass.AP(hstage, rv["ghs"], [[RING, 128], [1, HCOLS]]),
                hf[q][:],
            ).then_inc(sem_v, 1)
            V.reg_add(rv["ghs"], rv["ghs"], HCOLS)
            V.reg_add(rv["vxw"], rv["vxw"], 1)
            return emit_act_pool(s)

        def emit_act_pool(s):
            p = s % 2
            q = 1 - p
            A_.wait_ge(sem_v, ra["rv1"])
            A_.reg_add(ra["rv1"], ra["rv1"], 5)
            if "act" in GRU_SKIP:
                A_.nop().then_inc(sem_s, 1)
                A_.wait_ge(sem_g, ra["rga"])
                A_.reg_add(ra["rga"], ra["rga"], 3)
                A_.nop().then_inc(sem_s, 1)
            else:
                A_.activation(rzs[:], rz[:], AF.Sigmoid).then_inc(sem_s, 1)
                A_.wait_ge(sem_g, ra["rga"])
                A_.reg_add(ra["rga"], ra["rga"], 3)
                A_.activation(nn[:], t0b[:], AF.Tanh).then_inc(sem_s, 1)

            G.wait_ge(sem_v, rg["rv2"])
            G.reg_add(rg["rv2"], rg["rv2"], 5)
            if "pool" in GRU_SKIP:
                G.engine_nop().then_inc(sem_g, 1)
                G.reg_add(rg["gxw"], rg["gxw"], 1)
                G.wait_ge(sem_v, rg["rv3"]); G.reg_add(rg["rv3"], rg["rv3"], 5)
                G.engine_nop().then_inc(sem_g, 1)
                G.wait_ge(sem_v, rg["rv4"]); G.reg_add(rg["rv4"], rg["rv4"], 5)
                G.wait_ge(sem_pe, rg["rpe"]); G.reg_add(rg["rpe"], rg["rpe"], 2)
                G.engine_nop().then_inc(sem_g, 1)
                return
            G.tensor_add(
                t0b[:], t0[:],
                bass.AP(xwin, 16 * BC * GWIN, [[XWC, 128], [BC * GWIN, 8], [GWIN, BC]]) if STATIC_XW else
                bass.AP(xwin, rg["gxw"],
                        [[XWC, 128], [BC * GWIN, 8], [GWIN, BC]]),
            ).then_inc(sem_g, 1)
            G.reg_add(rg["gxw"], rg["gxw"], 1)
            G.wait_ge(sem_v, rg["rv3"])
            G.reg_add(rg["rv3"], rg["rv3"], 5)
            G.tensor_mul(d2[:], d1[:], rzs[:, HCOLS:2 * HCOLS]).then_inc(sem_g, 1)
            G.wait_ge(sem_v, rg["rv4"])
            G.reg_add(rg["rv4"], rg["rv4"], 5)
            G.wait_ge(sem_pe, rg["rpe"])
            G.reg_add(rg["rpe"], rg["rpe"], 2)
            G.tensor_copy(hb[q][:], hf[q][:]).then_inc(sem_g, 1)

        for w in range(NW):
            s0 = w * GWIN
            par_w = w % 2
            if w == 0 and "init" not in GRU_STRIP:
                PE.wait_ge(s_init2, 1)
            if "xwl" not in GRU_STRIP:
                V.wait_ge(s_xwl, 16 * 48 * (w + 1))
                G.wait_ge(s_xwl, 16 * 48 * (w + 1))
                PE.wait_ge(s_xwl, 16 * 48 * (w + 1))
            if w >= 1 and "flush" not in GRU_STRIP:
                G.wait_ge(sem_v, 5 * s0)
                G.dma_start(
                    out=hs_o[:, (w - 1) * RING:w * RING],
                    in_=hstage[:, :],
                ).then_inc(s_hsd, 16)
                V.wait_ge(s_hsd, 16 * w)
                if "loads" not in GRU_STRIP:
                    SP.wait_ge(sem_v, 5 * s0)
                    SP.wait_ge(sem_g, 3 * s0)
                    load_window(w)
            PE.reg_mov(r_pe_g, 3 * s0)
            V.reg_mov(rv["rp1"], 2 * s0 + 1)
            V.reg_mov(rv["rp2"], 2 * s0 + 2)
            V.reg_mov(rv["rs1"], 2 * s0 + 1)
            V.reg_mov(rv["rs2"], 2 * s0 + 2)
            V.reg_mov(rv["rgv"], 3 * s0 + 2)
            V.reg_mov(rv["vxw"], 0)
            V.reg_mov(rv["rva"], 5 * s0 + 4)
            V.reg_mov(rv["rfw"], 5 * s0)
            V.reg_mov(rv["ghs"], 0)
            A_.reg_mov(ra["rv1"], 5 * s0 + 1)
            A_.reg_mov(ra["rga"], 3 * s0 + 1)
            G.reg_mov(rg["rv2"], 5 * s0 + 2)
            G.reg_mov(rg["rv3"], 5 * s0 + 3)
            G.reg_mov(rg["rv4"], 5 * s0 + 4)
            G.reg_mov(rg["rpe"], 2 * s0)
            G.reg_mov(rg["gxw"], 16 * BC * GWIN)
            with (my_fori(GWIN // 2) if not USE_NCFORI else nc.Fori(0, GWIN // 2)):
                emit_step(s0)
                emit_step(s0 + 1)

        if "flush" in GRU_STRIP:
            return nc
        G.wait_ge(sem_v, 5 * S)
        G.dma_start(
            out=hs_o[:, (NW - 1) * RING:NW * RING],
            in_=hstage[:, :],
        ).then_inc(s_hsd, 16)
        G.wait_ge(s_hsd, 16 * NW)
        SP.wait_ge(s_xwl, 16 * xw_loads)

    return nc


def gru_in_maps(inputs, xw_maps, h0_maps):
    whh_t = host_tiles(np.ascontiguousarray(inputs["L0_ffn_gru_w_hh"].T), 8, 24)
    maps = []
    for c in range(NCORES):
        maps.append({"whh_t": whh_t, "h0_i": h0_maps[c], "xw_i": xw_maps[c]})
    return maps


CH = 256


def build_gru_chunk():
    nc = bass.Bass("TRN2", target_bir_lowering=False, num_devices=NCORES)

    whh_t = nc.dram_tensor("whh_t", [128, 24 * 8 * 128], F32, kind="ExternalInput")
    h_in = nc.dram_tensor("h_in", [128, HCOLS], F32, kind="ExternalInput")
    xw_i = nc.dram_tensor("xw_i", [24, BC, 128, CH], BF16, kind="ExternalInput")
    hs_o = nc.dram_tensor("hs_o", [128, CH * HCOLS], BF16, kind="ExternalOutput")
    h_out = nc.dram_tensor("h_out", [128, HCOLS], F32, kind="ExternalOutput")

    p = Prog(nc)
    import contextlib
    with contextlib.ExitStack() as st:
        whh = st.enter_context(nc.sbuf_tensor("whh", [128, 24576], BF16))
        xwin = st.enter_context(nc.sbuf_tensor("xwin", [128, 24 * BC * CH], BF16))
        hb0 = st.enter_context(nc.sbuf_tensor("hb0", [128, HCOLS], BF16))
        hb1 = st.enter_context(nc.sbuf_tensor("hb1", [128, HCOLS], BF16))
        hf0 = st.enter_context(nc.sbuf_tensor("hf0", [128, HCOLS], F32))
        hf1 = st.enter_context(nc.sbuf_tensor("hf1", [128, HCOLS], F32))
        rz = st.enter_context(nc.sbuf_tensor("rz", [128, 2 * HCOLS], F32))
        rzs = st.enter_context(nc.sbuf_tensor("rzs", [128, 2 * HCOLS], F32))
        t0 = st.enter_context(nc.sbuf_tensor("t0", [128, HCOLS], F32))
        t0b = st.enter_context(nc.sbuf_tensor("t0b", [128, HCOLS], F32))
        nn = st.enter_context(nc.sbuf_tensor("nn", [128, HCOLS], F32))
        d1 = st.enter_context(nc.sbuf_tensor("d1", [128, HCOLS], F32))
        d2 = st.enter_context(nc.sbuf_tensor("d2", [128, HCOLS], F32))
        hsb = st.enter_context(nc.sbuf_tensor("hsb", [128, CH * HCOLS], BF16))
        ps0 = st.enter_context(nc.psum_tensor("ps0", [128, PCOLS], F32))
        ps1 = st.enter_context(nc.psum_tensor("ps1", [128, PCOLS], F32))
        hb = [hb0, hb1]
        hf = [hf0, hf1]
        ps = [ps0, ps1]

        p.op("gpsimd", lambda e: e.dma_start(out=whh[:], in_=whh_t[:, :]),
             writes=["whh"])
        p.op("gpsimd", lambda e: e.dma_start(out=hf0[:], in_=h_in[:, :]),
             writes=[("hf", 0)])
        p.op("sync", lambda e: e.dma_start(
            out=xwin[:].rearrange("p (c b t) -> p c b t", c=24, b=BC),
            in_=xw_i.ap().rearrange("c b p t -> p c b t")),
             writes=["xw"])
        p.op("vector", lambda e: e.tensor_copy(hb0[:], hf0[:]),
             reads=[("hf", 0)], writes=[("hb", 0)])

        for s in range(CH):
            par = s % 2
            q = 1 - par
            for ej in range(24):
                for ki in range(8):
                    p.op("tensor", (lambda e, ej=ej, ki=ki, par=par: e.matmul(
                        ps[par][:, ej * BC:(ej + 1) * BC],
                        whh[:, (ej * 8 + ki) * 128:(ej * 8 + ki + 1) * 128],
                        hb[par][:, ki * BC:(ki + 1) * BC],
                        start=(ki == 0), stop=(ki == 7))),
                        reads=["whh", ("hb", par)] if ki == 0 and ej == 0 else
                              [("hb", par)],
                        writes=[("ps", par)] if ki == 7 and ej == 23 else [])
            xv = lambda lo, n, s=s: bass.AP(
                xwin, s + 0, [[24 * BC * CH, 128], [BC * CH, n], [CH, BC]]
            ) if lo == 0 else bass.AP(
                xwin, 16 * BC * CH + s, [[24 * BC * CH, 128], [BC * CH, n], [CH, BC]])
            p.op("vector", (lambda e, par=par, s=s: e.tensor_add(
                rz[:].rearrange("p (c b) -> p c b", c=16),
                ps[par][:, 0:2 * HCOLS].rearrange("p (c b) -> p c b", c=16),
                bass.AP(xwin, s, [[24 * BC * CH, 128], [BC * CH, 16], [CH, BC]]))),
                reads=[("ps", par), "xw"], writes=["rz"])
            p.op("scalar", lambda e: e.activation(rzs[:], rz[:], AF.Sigmoid),
                 reads=["rz"], writes=["rzs"])
            p.op("vector", (lambda e, par=par: e.tensor_mul(
                t0[:], rzs[:, 0:HCOLS], ps[par][:, 2 * HCOLS:3 * HCOLS])),
                reads=["rzs", ("ps", par)], writes=["t0"])
            p.op("vector", (lambda e, s=s: e.tensor_add(
                t0b[:].rearrange("p (c b) -> p c b", c=8),
                t0[:].rearrange("p (c b) -> p c b", c=8),
                bass.AP(xwin, 16 * BC * CH + s,
                        [[24 * BC * CH, 128], [BC * CH, 8], [CH, BC]]))),
                reads=["t0", "xw"], writes=["t0b"])
            p.op("scalar", lambda e: e.activation(nn[:], t0b[:], AF.Tanh),
                 reads=["t0b"], writes=["nn"])
            p.op("vector", (lambda e, par=par: e.tensor_sub(
                d1[:], hf[par][:], nn[:])),
                reads=[("hf", par), "nn"], writes=["d1"])
            p.op("vector", lambda e: e.tensor_mul(
                d2[:], d1[:], rzs[:, HCOLS:2 * HCOLS]),
                reads=["d1", "rzs"], writes=["d2"])
            p.op("vector", (lambda e, q=q: e.tensor_add(
                hf[q][:], nn[:], d2[:])),
                reads=["nn", "d2"], writes=[("hf", q)])
            p.op("vector", (lambda e, q=q, s=s: e.tensor_copy(
                hsb[:, s * HCOLS:(s + 1) * HCOLS], hf[q][:])),
                reads=[("hf", q)], writes=[("hsb", s)])
            p.op("scalar", (lambda e, q=q: e.activation(
                hb[q][:], hf[q][:], AF.Copy)),
                reads=[("hf", q)], writes=[("hb", q)])

        p.op("gpsimd", lambda e: e.dma_start(out=hs_o[:, :], in_=hsb[:]),
             reads=[("hsb", s) for s in range(CH)])
        p.op("gpsimd", (lambda e: e.dma_start(
            out=h_out[:, :], in_=hf[CH % 2][:])),
            reads=[("hf", CH % 2)])
        p.build()
    return nc


def build_moe():
    nc = bass.Bass("TRN2", target_bir_lowering=False, num_devices=NCORES)

    hs_i = nc.dram_tensor("hs_i", [128, S * HCOLS], BF16, kind="ExternalInput")
    expr_i = nc.dram_tensor("expr_i", [BC, DC, 128, S], BF16, kind="ExternalInput")
    xn_i = nc.dram_tensor("xn_i", [BC, DC, 128, S], BF16, kind="ExternalInput")
    smix_i = nc.dram_tensor("smix_i", [BC, DC, 128, S], F32, kind="ExternalInput")
    sh_i = nc.dram_tensor("sh_i", [BC, DC, 128, S], F32, kind="ExternalInput")
    x_i = nc.dram_tensor("x_i", [BC, DC, 128, S], F32, kind="ExternalInput")
    wg_i = nc.dram_tensor("wg_i", [E, 128, 16 * 8 * 128], F32, kind="ExternalInput")
    wu_i = nc.dram_tensor("wu_i", [E, 128, 16 * 8 * 128], F32, kind="ExternalInput")
    wd_i = nc.dram_tensor("wd_i", [E, 128, 8 * 16 * 128], F32, kind="ExternalInput")

    x1_o = nc.dram_tensor("x1_o", [BC, DC, 128, S], F32, kind="ExternalOutput")

    cos_d = nc.dram_tensor("cos_d", [E, BC, S], F32, kind="Internal")
    nrn_d = nc.dram_tensor("nrn_d", [E, BC, S], F32, kind="Internal")
    cw_d = nc.dram_tensor("cw_d", [E, BC, S], F32, kind="Internal")
    acc_d = nc.dram_tensor("acc_d", [BC, DC, 128, S], F32, kind="Internal")
    pen_in = nc.dram_tensor("pen_in", [1, 1], F32, kind="Internal")
    pen_out = nc.dram_tensor("pen_out", [1, 1], F32, kind="Internal")

    TTm = 256
    NTR = S // TTm

    p = Prog(nc)
    import contextlib
    with contextlib.ExitStack() as st:
        hsb = st.enter_context(nc.sbuf_tensor("hsb", [128, TTm * HCOLS], BF16))
        rnb = st.enter_context(nc.sbuf_tensor("rnb", [128, TTm * HCOLS], F32))
        exb = st.enter_context(nc.sbuf_tensor("exb", [128, DC * TTm], BF16))
        sqb = st.enter_context(nc.sbuf_tensor("sqb", [128, TTm * 2], F32))
        prod = st.enter_context(nc.sbuf_tensor("prod", [128, TTm], F32))
        rowZ = st.enter_context(nc.sbuf_tensor("rowZ", [1, 8 * TTm], F32))
        rowG = st.enter_context(nc.sbuf_tensor("rowG", [1, TTm], F32))
        rowW = st.enter_context(nc.sbuf_tensor("rowW", [1, 8 * TTm], F32))
        rowV = st.enter_context(nc.sbuf_tensor("rowV", [1, 8 * TTm], F32))
        pacc = st.enter_context(nc.sbuf_tensor("pacc", [1, 1], F32))
        ptmp = st.enter_context(nc.sbuf_tensor("ptmp", [1, TTm], F32))
        ones_col = st.enter_context(nc.sbuf_tensor("ones_col", [1, 128], F32))
        ones_vec = st.enter_context(nc.sbuf_tensor("ones_vec", [128, 1], F32))
        psR = st.enter_context(nc.psum_tensor("psR", [1, 512], F32))
        psS = st.enter_context(nc.psum_tensor("psS", [1, 512], F32))
        psB = st.enter_context(nc.psum_tensor("psB", [128, 512], F32))
        psC = st.enter_context(nc.psum_tensor("psC", [128, 512], F32))
        wgb = st.enter_context(nc.sbuf_tensor("wgb", [128, 16384], BF16))
        wub = st.enter_context(nc.sbuf_tensor("wub", [128, 16384], BF16))
        wdb = st.enter_context(nc.sbuf_tensor("wdb", [128, 16384], BF16))
        xnt = st.enter_context(nc.sbuf_tensor("xnt", [128, DC * TT], BF16))
        hid = st.enter_context(nc.sbuf_tensor("hid", [128, FC * TT], BF16))
        ev1 = st.enter_context(nc.sbuf_tensor("ev1", [128, 2 * TT], F32))
        ev2 = st.enter_context(nc.sbuf_tensor("ev2", [128, TT], F32))
        cwrow = st.enter_context(nc.sbuf_tensor("cwrow", [1, TT], F32))

        p.op("vector", lambda e: e.memset(ones_col[:], 1.0), writes=["ones"])
        p.op("vector", lambda e: e.memset(ones_vec[:], 1.0), writes=["onesv"])
        p.op("vector", lambda e: e.memset(pacc[:], 0.0), writes=["pacc"])

        EPSN = EPS_RMS

        for b in range(BC):
            for rt in range(NTR):
                t0_ = rt * TTm
                p.op("sync", (lambda e, b=b, t0_=t0_: e.dma_start(
                    out=hsb[:].rearrange("p (t cb) -> p t cb", cb=16),
                    in_=hs_i.ap().rearrange("p (t c bb) -> p t (c bb)", t=S, c=DC)[
                        :, t0_:t0_ + TTm, :])),
                    writes=["hsb"])
                for c in range(DC):
                    p.op("scalar", (lambda e, c=c, b=b: e.activation(
                        sqb[:, 0:TTm],
                        bass.AP(hsb, c * BC + b, [[TTm * 16, 128], [16, TTm]]),
                        AF.Square)),
                        reads=["hsb"], writes=["sqb"])
                    p.op("tensor", (lambda e, c=c: e.matmul(
                        psR[:, 0:TTm], ones_vec[:, :], sqb[:, 0:TTm],
                        start=True, stop=True)),
                        reads=["sqb", "onesv"], writes=["psRn"])
                    p.op("vector", (lambda e, c=c: e.tensor_copy(
                        rowZ[:, c * TTm:(c + 1) * TTm], psR[:, 0:TTm])),
                        reads=["psRn"], writes=[("rowZ", c)])
                for c in range(DC):
                    p.op("scalar", (lambda e, c=c: e.activation(
                        rowW[:, 0:TTm], rowZ[:, c * TTm:(c + 1) * TTm], AF.Sqrt)),
                        reads=[("rowZ", c)], writes=["roww"])
                    p.op("vector", lambda e: e.tensor_scalar_max(
                        rowW[:, TTm:2 * TTm], rowW[:, 0:TTm], EPSN),
                        reads=["roww"], writes=["roww2"])
                    p.op("vector", lambda e: e.reciprocal(
                        rowW[:, 2 * TTm:3 * TTm], rowW[:, TTm:2 * TTm]),
                        reads=["roww2"], writes=["roww3"])
                    p.op("vector", (lambda e, c=c: e.tensor_mul(
                        rowW[:, 3 * TTm:4 * TTm], rowW[:, 0:TTm],
                        rowW[:, 2 * TTm:3 * TTm])),
                        reads=["roww", "roww3"], writes=["roww4"])
                    p.op("gpsimd", (lambda e, c=c, b=b, t0_=t0_: e.dma_start(
                        out=nrn_d.ap()[c, b, t0_:t0_ + TTm][None, :],
                        in_=rowW[:, 3 * TTm:4 * TTm])),
                        reads=["roww4"])
                    p.op("tensor", (lambda e, c=c: e.matmul(
                        psB[:, 0:TTm], ones_col[:, :], rowW[:, 2 * TTm:3 * TTm],
                        start=True, stop=True)),
                        reads=["roww3", "ones"], writes=["psBb"])
                    p.op("vector", (lambda e, c=c, b=b: e.tensor_mul(
                        rnb[:, c * TTm:(c + 1) * TTm],
                        bass.AP(hsb, c * BC + b, [[TTm * 16, 128], [16, TTm]]),
                        psB[:, 0:TTm])),
                        reads=["hsb", "psBb"], writes=[("rnb", c)])
                p.op("sync", (lambda e, b=b, t0_=t0_: e.dma_start(
                    out=exb[:].rearrange("p (c f) -> p c f", c=DC),
                    in_=expr_i.ap()[b, :, :, t0_:t0_ + TTm].rearrange(
                        "c p f -> p c f"))),
                    writes=["exb"])
                for c in range(DC):
                    p.op("vector", (lambda e, c=c: e.tensor_mul(
                        prod[:], exb[:, c * TTm:(c + 1) * TTm],
                        rnb[:, c * TTm:(c + 1) * TTm])),
                        reads=["exb", ("rnb", c)], writes=["prod"])
                    p.op("tensor", (lambda e, c=c: e.matmul(
                        psR[:, 0:TTm], ones_vec[:, :], prod[:],
                        start=True, stop=True)),
                        reads=["prod", "onesv"], writes=["psRn"])
                    p.op("vector", (lambda e, c=c: e.tensor_copy(
                        rowV[:, c * TTm:(c + 1) * TTm], psR[:, 0:TTm])),
                        reads=["psRn"], writes=[("rowV", c)])
                    p.op("scalar", (lambda e, c=c: e.activation(
                        sqb[:, 0:TTm], exb[:, c * TTm:(c + 1) * TTm], AF.Square)),
                        reads=["exb"], writes=["sqb"])
                    p.op("tensor", (lambda e, c=c: e.matmul(
                        psS[:, 0:TTm], ones_vec[:, :], sqb[:, 0:TTm],
                        start=True, stop=True)),
                        reads=["sqb", "onesv"], writes=["psSn"])
                    p.op("scalar", (lambda e, c=c: e.activation(
                        rowW[:, 0:TTm], psS[:, 0:TTm], AF.Sqrt)),
                        reads=["psSn"], writes=["roww"])
                    p.op("vector", lambda e: e.tensor_scalar_max(
                        rowW[:, TTm:2 * TTm], rowW[:, 0:TTm], EPS_COS),
                        reads=["roww"], writes=["roww2"])
                    p.op("vector", (lambda e, c=c: e.tensor_scalar_max(
                        rowW[:, 2 * TTm:3 * TTm], rowW[:, 3 * TTm:4 * TTm]
                        if False else rowW[:, 2 * TTm:3 * TTm], EPS_COS)),
                        reads=["roww3"], writes=["dummy_unused"]) if False else None
                    p.op("sync", (lambda e, c=c, b=b, t0_=t0_: e.dma_start(
                        out=rowW[:, 2 * TTm:3 * TTm],
                        in_=nrn_d.ap()[c, b, t0_:t0_ + TTm][None, :])),
                        writes=["roww3"])
                    p.op("vector", lambda e: e.tensor_scalar_max(
                        rowW[:, 3 * TTm:4 * TTm], rowW[:, 2 * TTm:3 * TTm], EPS_COS),
                        reads=["roww3"], writes=["roww4"])
                    p.op("vector", lambda e: e.tensor_mul(
                        rowW[:, 2 * TTm:3 * TTm], rowW[:, TTm:2 * TTm],
                        rowW[:, 3 * TTm:4 * TTm]),
                        reads=["roww2", "roww4"], writes=["roww3"])
                    p.op("vector", lambda e: e.reciprocal(
                        rowW[:, TTm:2 * TTm], rowW[:, 2 * TTm:3 * TTm]),
                        reads=["roww3"], writes=["roww2"])
                    p.op("vector", (lambda e, c=c: e.tensor_mul(
                        rowW[:, 0:TTm], rowV[:, c * TTm:(c + 1) * TTm],
                        rowW[:, TTm:2 * TTm])),
                        reads=[("rowV", c), "roww2"], writes=["roww"])
                    p.op("vector", lambda e: e.scalar_tensor_tensor(
                        rowW[:, TTm:2 * TTm], rowW[:, 0:TTm], -1.0,
                        rowW[:, 0:TTm], ALU.mult, ALU.subtract),
                        reads=["roww"], writes=["roww2"]) if False else None
                    p.op("vector", lambda e: e.tensor_scalar(
                        rowW[:, TTm:2 * TTm], rowW[:, 0:TTm], -1.0, 1.0,
                        ALU.mult, ALU.add),
                        reads=["roww"], writes=["roww2"])
                    p.op("gpsimd", (lambda e, c=c, b=b, t0_=t0_: e.dma_start(
                        out=cos_d.ap()[c, b, t0_:t0_ + TTm][None, :],
                        in_=rowW[:, TTm:2 * TTm])),
                        reads=["roww2"])
                for e1 in range(E):
                    for f1 in range(E):
                        p.op("vector", (lambda e, e1=e1, f1=f1: e.tensor_mul(
                            prod[:], rnb[:, e1 * TTm:(e1 + 1) * TTm],
                            rnb[:, f1 * TTm:(f1 + 1) * TTm])),
                            reads=[("rnb", e1), ("rnb", f1)], writes=["prod"])
                        p.op("tensor", (lambda e: e.matmul(
                            psR[:, 0:TTm], ones_vec[:, :], prod[:],
                            start=True, stop=True)),
                            reads=["prod", "onesv"], writes=["psRn"])
                        if e1 == f1:
                            p.op("vector", lambda e: e.tensor_scalar_add(
                                rowG[:, 0:TTm], psR[:, 0:TTm], -1.0),
                                reads=["psRn"], writes=["rowg"])
                            p.op("scalar", lambda e: e.activation(
                                rowG[:, 0:TTm], rowG[:, 0:TTm], AF.Square),
                                reads=["rowg"], writes=["rowg"])
                        else:
                            p.op("scalar", lambda e: e.activation(
                                rowG[:, 0:TTm], psR[:, 0:TTm], AF.Square),
                                reads=["psRn"], writes=["rowg"])
                        if f1 == 0:
                            p.op("vector", lambda e: e.tensor_copy(
                                rowW[:, 0:TTm], rowG[:, 0:TTm]),
                                reads=["rowg"], writes=["roww"])
                        else:
                            p.op("vector", lambda e: e.tensor_add(
                                rowW[:, 0:TTm], rowW[:, 0:TTm], rowG[:, 0:TTm]),
                                reads=["rowg", "roww"], writes=["roww"])
                    p.op("scalar", lambda e: e.activation(
                        rowW[:, TTm:2 * TTm], rowW[:, 0:TTm], AF.Sqrt),
                        reads=["roww"], writes=["roww2"])
                    p.op("vector", lambda e: e.tensor_scalar_max(
                        rowW[:, 2 * TTm:3 * TTm], rowW[:, TTm:2 * TTm], EPSN),
                        reads=["roww2"], writes=["roww3"])
                    p.op("vector", lambda e: e.reciprocal(
                        rowW[:, TTm:2 * TTm], rowW[:, 2 * TTm:3 * TTm]),
                        reads=["roww3"], writes=["roww2"])
                    p.op("scalar", lambda e: e.activation(
                        rowW[:, 2 * TTm:3 * TTm], rowW[:, TTm:2 * TTm], AF.Square),
                        reads=["roww2"], writes=["roww3"])
                    p.op("vector", lambda e: e.tensor_mul(
                        rowW[:, TTm:2 * TTm], rowW[:, 0:TTm],
                        rowW[:, 2 * TTm:3 * TTm]),
                        reads=["roww", "roww3"], writes=["roww2"])
                    p.op("vector", lambda e: e.tensor_reduce(
                        ptmp[:, 0:1], rowW[:, TTm:2 * TTm], AX.X, ALU.add),
                        reads=["roww2"], writes=["ptmp"])
                    p.op("vector", lambda e: e.tensor_add(
                        pacc[:], pacc[:], ptmp[:, 0:1]),
                        reads=["ptmp", "pacc"], writes=["pacc"])

        p.op("gpsimd", lambda e: e.dma_start(out=pen_in[:, :], in_=pacc[:]),
             reads=["pacc"], writes=["pen_in_d"])
        p.op("gpsimd", lambda e: e.collective_compute(
            "AllReduce", ALU.add, replica_groups=[list(range(NCORES))],
            ins=[pen_in.ap().opt()], outs=[pen_out.ap().opt()]),
            reads=["pen_in_d"], writes=["pen_out_d"], cc=True)
        p.op("sync", lambda e: e.dma_start(out=pacc[:], in_=pen_out[:, :]),
             reads=["pen_out_d"], writes=["pacc"])
        p.op("vector", lambda e: e.tensor_scalar(
            pacc[:], pacc[:], 1.0 / (16 * S), 1.0, ALU.mult, ALU.add),
            reads=["pacc"], writes=["pacc"])

        for b in range(BC):
            for rt in range(NTR):
                t0_ = rt * TTm
                for c in range(E):
                    p.op("sync", (lambda e, c=c, b=b, t0_=t0_: e.dma_start(
                        out=rowV[:, c * TTm:(c + 1) * TTm],
                        in_=cos_d.ap()[c, b, t0_:t0_ + TTm][None, :])),
                        writes=[("rowV", c)])
                    p.op("vector", (lambda e, c=c: e.tensor_scalar_mul(
                        rowV[:, c * TTm:(c + 1) * TTm],
                        rowV[:, c * TTm:(c + 1) * TTm], pacc[0:1, 0:1])),
                        reads=[("rowV", c), "pacc"], writes=[("rowV", c)])
                p.op("vector", lambda e: e.tensor_max(
                    rowW[:, 0:TTm], rowV[:, 0:TTm], rowV[:, TTm:2 * TTm]),
                    reads=[("rowV", 0), ("rowV", 1)], writes=["roww"])
                for c in range(2, E):
                    p.op("vector", (lambda e, c=c: e.tensor_max(
                        rowW[:, 0:TTm], rowW[:, 0:TTm],
                        rowV[:, c * TTm:(c + 1) * TTm])),
                        reads=["roww", ("rowV", c)], writes=["roww"])
                first = True
                for c in range(E):
                    p.op("vector", (lambda e, c=c: e.tensor_tensor(
                        rowW[:, TTm:2 * TTm], rowV[:, c * TTm:(c + 1) * TTm],
                        rowW[:, 0:TTm], ALU.is_ge)),
                        reads=[("rowV", c), "roww"], writes=["roww2"])
                    p.op("vector", (lambda e, c=c: e.scalar_tensor_tensor(
                        rowW[:, 2 * TTm:3 * TTm], rowW[:, TTm:2 * TTm], -1e30,
                        rowV[:, c * TTm:(c + 1) * TTm], ALU.mult, ALU.add)),
                        reads=["roww2", ("rowV", c)], writes=["roww3"])
                    if first:
                        p.op("vector", lambda e: e.tensor_copy(
                            rowW[:, 3 * TTm:4 * TTm], rowW[:, 2 * TTm:3 * TTm]),
                            reads=["roww3"], writes=["roww4"])
                        first = False
                    else:
                        p.op("vector", lambda e: e.tensor_max(
                            rowW[:, 3 * TTm:4 * TTm], rowW[:, 3 * TTm:4 * TTm],
                            rowW[:, 2 * TTm:3 * TTm]),
                            reads=["roww3", "roww4"], writes=["roww4"])
                p.op("vector", lambda e: e.tensor_sub(
                    rowW[:, TTm:2 * TTm], rowW[:, 3 * TTm:4 * TTm], rowW[:, 0:TTm]),
                    reads=["roww4", "roww"], writes=["roww2"])
                p.op("scalar", lambda e: e.activation(
                    rowW[:, 2 * TTm:3 * TTm], rowW[:, TTm:2 * TTm], AF.Exp),
                    reads=["roww2"], writes=["roww3"])
                p.op("vector", lambda e: e.tensor_scalar_add(
                    rowW[:, 2 * TTm:3 * TTm], rowW[:, 2 * TTm:3 * TTm], 1.0),
                    reads=["roww3"], writes=["roww3"])
                p.op("vector", lambda e: e.reciprocal(
                    rowW[:, TTm:2 * TTm], rowW[:, 2 * TTm:3 * TTm]),
                    reads=["roww3"], writes=["roww2"])
                for c in range(E):
                    p.op("vector", (lambda e, c=c: e.tensor_sub(
                        prod[0:1, 0:TTm], rowV[:, c * TTm:(c + 1) * TTm],
                        rowW[:, 0:TTm])),
                        reads=[("rowV", c), "roww"], writes=["prodr"])
                    p.op("scalar", lambda e: e.activation(
                        prod[0:1, TTm:2 * TTm] if False else ptmp[:, 0:TTm],
                        prod[0:1, 0:TTm], AF.Exp),
                        reads=["prodr"], writes=["ptmp"])
                    p.op("vector", (lambda e, c=c: e.tensor_tensor(
                        prod[0:1, 0:TTm], rowV[:, c * TTm:(c + 1) * TTm],
                        rowW[:, 3 * TTm:4 * TTm], ALU.is_ge)),
                        reads=[("rowV", c), "roww4"], writes=["prodr"])
                    p.op("vector", lambda e: e.tensor_mul(
                        ptmp[:, 0:TTm], ptmp[:, 0:TTm], prod[0:1, 0:TTm]),
                        reads=["ptmp", "prodr"], writes=["ptmp"])
                    p.op("vector", lambda e: e.tensor_mul(
                        ptmp[:, 0:TTm], ptmp[:, 0:TTm], rowW[:, TTm:2 * TTm]),
                        reads=["ptmp", "roww2"], writes=["ptmp"])
                    p.op("gpsimd", (lambda e, c=c, b=b, t0_=t0_: e.dma_start(
                        out=cw_d.ap()[c, b, t0_:t0_ + TTm][None, :],
                        in_=ptmp[:, 0:TTm])),
                        reads=["ptmp"])

        for ex in range(E):
            p.op("gpsimd", (lambda e, ex=ex: e.dma_start(
                out=wgb[:], in_=wg_i.ap()[ex])), writes=["wgb"])
            p.op("gpsimd", (lambda e, ex=ex: e.dma_start(
                out=wub[:], in_=wu_i.ap()[ex])), writes=["wub"])
            p.op("gpsimd", (lambda e, ex=ex: e.dma_start(
                out=wdb[:], in_=wd_i.ap()[ex])), writes=["wdb"])
            for b in range(BC):
                for t in range(TPB):
                    p.op("sync", (lambda e, b=b, t=t: e.dma_start(
                        out=xnt[:].rearrange("p (c f) -> p c f", c=DC),
                        in_=xn_i.ap()[b, :, :, t * TT:(t + 1) * TT].rearrange(
                            "c p f -> p c f"))),
                        writes=["xnt"])
                    for f in range(FC):
                        for k in range(DC):
                            p.op("tensor", (lambda e, f=f, k=k: e.matmul(
                                psB[:, 0:TT],
                                wgb[:, (f * 8 + k) * 128:(f * 8 + k + 1) * 128],
                                xnt[:, ts(k, TT)],
                                start=(k == 0), stop=(k == DC - 1))),
                                reads=["wgb", "xnt"], writes=["psBm"])
                        p.op("scalar", lambda e: e.activation(
                            ev1[:, 0:TT], psB[:, 0:TT], AF.Silu),
                            reads=["psBm"], writes=["evg"])
                        for k in range(DC):
                            p.op("tensor", (lambda e, f=f, k=k: e.matmul(
                                psC[:, 0:TT],
                                wub[:, (f * 8 + k) * 128:(f * 8 + k + 1) * 128],
                                xnt[:, ts(k, TT)],
                                start=(k == 0), stop=(k == DC - 1))),
                                reads=["wub", "xnt"], writes=["psCm"])
                        p.op("vector", (lambda e, f=f: e.tensor_mul(
                            hid[:, f * TT:(f + 1) * TT], ev1[:, 0:TT], psC[:, 0:TT])),
                            reads=["evg", "psCm"], writes=[("hid", f)])
                    p.op("sync", (lambda e, ex=ex, b=b, t=t: e.dma_start(
                        out=cwrow[:, 0:TT],
                        in_=cw_d.ap()[ex, b, t * TT:(t + 1) * TT][None, :])),
                        writes=["cwrow"])
                    p.op("tensor", lambda e: e.matmul(
                        psC[:, 0:TT], ones_col[:, :], cwrow[:, 0:TT],
                        start=True, stop=True),
                        reads=["cwrow", "ones"], writes=["psCm"])
                    for c in range(DC):
                        for k in range(FC):
                            p.op("tensor", (lambda e, c=c, k=k: e.matmul(
                                psB[:, 0:TT],
                                wdb[:, (c * FC + k) * 128:(c * FC + k + 1) * 128],
                                hid[:, k * TT:(k + 1) * TT],
                                start=(k == 0), stop=(k == FC - 1))),
                                reads=["wdb", ("hid", k)], writes=["psBm"])
                        p.op("vector", (lambda e, c=c: e.tensor_mul(
                            ev2[:], psB[:, 0:TT], psC[:, 0:TT])),
                            reads=["psBm", "psCm"], writes=["ev2"]) if False else None
                        p.op("scalar", (lambda e, c=c: e.activation(
                            ev1[:, TT:2 * TT], psB[:, 0:TT], AF.Copy)),
                            reads=["psBm"], writes=["evm"])
                        p.op("vector", (lambda e, c=c: e.tensor_mul(
                            ev2[:], ev1[:, TT:2 * TT], psC[:, 0:TT])),
                            reads=["evm", "psCm"], writes=["ev2"])
                        if ex == 0:
                            p.op("sync", (lambda e, c=c, b=b, t=t: e.dma_start(
                                out=ev1[:, 0:TT],
                                in_=sh_i.ap()[b, c, :, t * TT:(t + 1) * TT])),
                                writes=["evg"])
                            p.op("vector", lambda e: e.tensor_add(
                                ev2[:], ev2[:], ev1[:, 0:TT]),
                                reads=["ev2", "evg"], writes=["ev2"])
                            p.op("gpsimd", (lambda e, c=c, b=b, t=t: e.dma_start(
                                out=acc_d.ap()[b, c, :, t * TT:(t + 1) * TT],
                                in_=ev2[:])),
                                reads=["ev2"], writes=[("acc", b, c, t)])
                        else:
                            p.op("gpsimd", (lambda e, c=c, b=b, t=t: e.dma_start(
                                out=acc_d.ap()[b, c, :, t * TT:(t + 1) * TT],
                                in_=ev2[:], accum_op=ALU.add)),
                                reads=["ev2"], writes=[("acc", b, c, t)])

        for b in range(BC):
            for t in range(TPB):
                for c in range(DC):
                    p.op("sync", (lambda e, c=c, b=b, t=t: e.dma_start(
                        out=ev1[:, 0:TT],
                        in_=acc_d.ap()[b, c, :, t * TT:(t + 1) * TT])),
                        reads=[("acc", b, c, t)], writes=["evg"])
                    p.op("scalar", lambda e: e.activation(
                        ev1[:, TT:2 * TT], ev1[:, 0:TT], AF.Silu),
                        reads=["evg"], writes=["evm"])
                    p.op("sync", (lambda e, c=c, b=b, t=t: e.dma_start(
                        out=ev2[:],
                        in_=smix_i.ap()[b, c, :, t * TT:(t + 1) * TT])),
                        writes=["ev2"])
                    p.op("vector", lambda e: e.tensor_mul(
                        ev1[:, 0:TT], ev1[:, TT:2 * TT], ev2[:]),
                        reads=["evm", "ev2"], writes=["evg"])
                    p.op("sync", (lambda e, c=c, b=b, t=t: e.dma_start(
                        out=ev2[:],
                        in_=x_i.ap()[b, c, :, t * TT:(t + 1) * TT])),
                        writes=["ev2"])
                    p.op("vector", lambda e: e.tensor_add(
                        ev1[:, TT:2 * TT], ev1[:, 0:TT], ev2[:]),
                        reads=["evg", "ev2"], writes=["evm"])
                    p.op("gpsimd", (lambda e, c=c, b=b, t=t: e.dma_start(
                        out=x1_o.ap()[b, c, :, t * TT:(t + 1) * TT],
                        in_=ev1[:, TT:2 * TT])),
                        reads=["evm"])
        p.build()
    return nc


NB = S // 128
NH = 16
DH = 64


def build_l1():
    nc = bass.Bass("TRN2", target_bir_lowering=False, num_devices=NCORES)

    x1_i = nc.dram_tensor("x1_i", [BC, DC, 128, S], F32, kind="ExternalInput")
    wq_i = nc.dram_tensor("wq_i", [128, 8192], F32, kind="ExternalInput")
    wk_i = nc.dram_tensor("wk_i", [128, 8192], F32, kind="ExternalInput")
    wo_i = nc.dram_tensor("wo_i", [128, 8192], F32, kind="ExternalInput")
    wv_i = nc.dram_tensor("wv_i", [128, 8192], F32, kind="ExternalInput")
    bq_i = nc.dram_tensor("bq_i", [128, DC], F32, kind="ExternalInput")
    bk_i = nc.dram_tensor("bk_i", [128, DC], F32, kind="ExternalInput")
    bv_i = nc.dram_tensor("bv_i", [128, DC], F32, kind="ExternalInput")
    bvr_i = nc.dram_tensor("bvr_i", [1, 1024], F32, kind="ExternalInput")
    bo_i = nc.dram_tensor("bo_i", [128, DC], F32, kind="ExternalInput")
    wg_i = nc.dram_tensor("wg_i", [128, 24576], I32, kind="ExternalInput")
    wu_i = nc.dram_tensor("wu_i", [128, 24576], I32, kind="ExternalInput")
    wd_i = nc.dram_tensor("wd_i", [128, 24576], I32, kind="ExternalInput")

    out_o = nc.dram_tensor("out_o", [BC, DC, 128, S], F32, kind="ExternalOutput")
    xn1_d = nc.dram_tensor("xn1_d", [BC, DC, 128, S], BF16, kind="Internal")
    s1_d = nc.dram_tensor("s1_d", [BC, DC, 128, S], F32, kind="Internal")

    p = Prog(nc)
    import contextlib
    with contextlib.ExitStack() as st:
        UB1 = st.enter_context(nc.sbuf_tensor("UB1", [128, 49152], BF16))
        PW = st.enter_context(nc.sbuf_tensor("PW", [128, 32768], BF16))
        ints = st.enter_context(nc.sbuf_tensor("ints", [128, 512], I32))
        xnt = st.enter_context(nc.sbuf_tensor("xnt", [128, DC * TT], BF16))
        xf = st.enter_context(nc.sbuf_tensor("xf", [128, DC * 256], F32))
        xsq = st.enter_context(nc.sbuf_tensor("xsq", [128, 256], F32))
        rowA = st.enter_context(nc.sbuf_tensor("rowA", [1, 4 * 256], F32))
        ceps = st.enter_context(nc.sbuf_tensor("ceps", [128, 1], F32))
        bias4 = st.enter_context(nc.sbuf_tensor("bias4", [128, 4 * DC], F32))
        bvrow = st.enter_context(nc.sbuf_tensor("bvrow", [1, 1024], BF16))
        ones_colB = st.enter_context(nc.sbuf_tensor("ones_colB", [1, 128], BF16))
        bop = st.enter_context(nc.sbuf_tensor("bop", [128, DC], F32))
        mask = st.enter_context(nc.sbuf_tensor("mask", [128, 3 * 128], BF16))
        ident = st.enter_context(nc.sbuf_tensor("ident", [128, 128], BF16))
        probs = st.enter_context(nc.sbuf_tensor("probs", [128, 3 * 128], BF16))
        pT = st.enter_context(nc.sbuf_tensor("pT", [128, 3 * 128], BF16))
        att = st.enter_context(nc.sbuf_tensor("att", [128, 1024], BF16))
        attf = st.enter_context(nc.sbuf_tensor("attf", [128, 1024], BF16))
        TT2 = 256
        hid = st.enter_context(nc.sbuf_tensor("hid", [128, 24 * TT2], BF16))
        ev1 = st.enter_context(nc.sbuf_tensor("ev1", [128, 512], F32))
        ev2 = st.enter_context(nc.sbuf_tensor("ev2", [128, 256], F32))
        stat = st.enter_context(nc.sbuf_tensor("stat", [128, 4], F32))
        ones_col = st.enter_context(nc.sbuf_tensor("ones_col", [1, 128], F32))
        ones_vec = st.enter_context(nc.sbuf_tensor("ones_vec", [128, 1], F32))
        psA = st.enter_context(nc.psum_tensor("psA", [128, 512], F32))
        psB = st.enter_context(nc.psum_tensor("psB", [128, 512], F32))
        psT = st.enter_context(nc.psum_tensor("psT", [128, 1024], BF16))
        psR1 = st.enter_context(nc.psum_tensor("psR1", [1, 512], F32))
        psR2 = st.enter_context(nc.psum_tensor("psR2", [1, 512], F32))
        psC = st.enter_context(nc.psum_tensor("psC", [128, 512], F32))
        psD = st.enter_context(nc.psum_tensor("psD", [128, 512], F32))

        QO, KO, VO = 0, 16384, 32768

        p.op("vector", lambda e: e.memset(ones_col[:], 1.0), writes=["ones"])
        p.op("vector", lambda e: e.memset(ones_colB[:], 1.0), writes=["onesB"])
        p.op("vector", lambda e: e.memset(ones_vec[:], 1.0), writes=["onesv"])
        p.op("vector", lambda e: e.memset(ceps[:], EPS_LN), writes=["ceps"])
        p.op("vector", lambda e: e.memset(mask[:], 0.0), writes=["mask"])
        p.op("gpsimd", lambda e: e.affine_select(
            mask[:], mask[:], [[1, 384]], ALU.is_ge, -1e30,
            base=0, channel_multiplier=-1),
            reads=["mask"], writes=["mask"], gp_compute=True)
        p.op("gpsimd", lambda e: e.affine_select(
            mask[:], mask[:], [[-1, 384]], ALU.is_ge, -1e30,
            base=256, channel_multiplier=1),
            reads=["mask"], writes=["mask"], gp_compute=True)
        p.op("vector", lambda e: e.memset(ident[:], 1.0), writes=["ident"])
        p.op("gpsimd", lambda e: e.affine_select(
            ident[:], ident[:], [[1, 128]], ALU.is_equal, 0.0,
            base=0, channel_multiplier=-1) if False else e.affine_select(
            ident[:], ident[:], [[1, 128]], ALU.is_ge, 0.0,
            base=0, channel_multiplier=-1),
            reads=["ident"], writes=["ident"], gp_compute=True)
        p.op("gpsimd", lambda e: e.affine_select(
            ident[:], ident[:], [[-1, 128]], ALU.is_ge, 0.0,
            base=0, channel_multiplier=1),
            reads=["ident"], writes=["ident"], gp_compute=True)
        p.op("sync", lambda e: e.dma_start(out=bias4[:, 0:DC], in_=bq_i[:, :]),
             writes=["bq"])
        p.op("sync", lambda e: e.dma_start(out=bias4[:, DC:2 * DC], in_=bk_i[:, :]),
             writes=["bk"])
        p.op("sync", lambda e: e.dma_start(out=bias4[:, 2 * DC:3 * DC], in_=bv_i[:, :]),
             writes=["bv"])
        p.op("sync", lambda e: e.dma_start(out=bias4[:, 3 * DC:4 * DC], in_=bo_i[:, :]),
             writes=["bo"])
        p.op("gpsimd", lambda e: e.dma_start(out=bvrow[:], in_=bvr_i[:, :]),
             writes=["bvr"])
        p.op("gpsimd", lambda e: e.dma_start(out=PW[:, 0:8192], in_=wq_i[:, :]),
             writes=["wqt"])
        p.op("gpsimd", lambda e: e.dma_start(out=PW[:, 8192:16384], in_=wk_i[:, :]),
             writes=["wkt"])
        p.op("gpsimd", lambda e: e.dma_start(out=PW[:, 16384:24576], in_=wo_i[:, :]),
             writes=["wot"])
        p.op("gpsimd", lambda e: e.dma_start(out=PW[:, 24576:32768], in_=wv_i[:, :]),
             writes=["wvr"])
        p.op("vector", lambda e: e.tensor_copy(
            bop[:], bias4[:, 3 * DC:4 * DC]),
            reads=["bo"], writes=["bop"])

        TTL = 256
        for b in range(BC):
            for t in range(S // TTL):
                p.op("sync", (lambda e, b=b, t=t: e.dma_start(
                    out=xf[:].rearrange("p (c f) -> p c f", c=DC),
                    in_=x1_i.ap()[b, :, :, t * TTL:(t + 1) * TTL].rearrange(
                        "c p f -> p c f"))),
                    writes=["xf"])
                for k in range(DC):
                    p.op("tensor", (lambda e, k=k: e.matmul(
                        psR1[:, 0:TTL], ones_vec[:, :], xf[:, ts(k, TTL)],
                        start=(k == 0), stop=(k == DC - 1))),
                        reads=["xf", "onesv"], writes=["psR1"])
                p.op("scalar", lambda e: e.activation(
                    rowA[:, 0:TTL], psR1[:, 0:TTL], AF.Copy, scale=-1.0 / D),
                    reads=["psR1"], writes=["row0"])
                for k in range(DC):
                    p.op("scalar", (lambda e, k=k: e.activation(
                        xsq[:, 0:TTL], xf[:, ts(k, TTL)], AF.Square)),
                        reads=["xf"], writes=["xsq"])
                    p.op("tensor", (lambda e, k=k: e.matmul(
                        psR2[:, 0:TTL], ones_vec[:, :], xsq[:, 0:TTL],
                        start=(k == 0), stop=(k == DC - 1))),
                        reads=["xsq", "onesv"], writes=["psR2"])
                p.op("scalar", lambda e: e.activation(
                    rowA[:, TTL:2 * TTL], rowA[:, 0:TTL], AF.Square),
                    reads=["row0"], writes=["row1"])
                p.op("vector", lambda e: e.scalar_tensor_tensor(
                    rowA[:, 2 * TTL:3 * TTL], psR2[:, 0:TTL], 1.0 / D,
                    rowA[:, TTL:2 * TTL], ALU.mult, ALU.subtract),
                    reads=["psR2", "row1"], writes=["row2"])
                p.op("scalar", lambda e: e.activation(
                    rowA[:, 3 * TTL:4 * TTL], rowA[:, 2 * TTL:3 * TTL], AF.Ln,
                    bias=ceps[0:1, 0:1]),
                    reads=["row2", "ceps"], writes=["row3"])
                p.op("scalar", lambda e: e.activation(
                    rowA[:, 2 * TTL:3 * TTL], rowA[:, 3 * TTL:4 * TTL], AF.Exp,
                    scale=-0.5),
                    reads=["row3"], writes=["row2"])
                p.op("tensor", lambda e: e.matmul(
                    psC[:, 0:TTL], ones_col[:, :], rowA[:, 0:TTL],
                    start=True, stop=True),
                    reads=["row0", "ones"], writes=["psC"])
                p.op("tensor", lambda e: e.matmul(
                    psD[:, 0:TTL], ones_col[:, :], rowA[:, 2 * TTL:3 * TTL],
                    start=True, stop=True),
                    reads=["row2", "ones"], writes=["psD"])
                for k in range(DC):
                    p.op("vector", (lambda e, k=k: e.tensor_add(
                        ev1[:, 0:TTL], xf[:, ts(k, TTL)], psC[:, 0:TTL])),
                        reads=["xf", "psC"], writes=["ev1a"])
                    p.op("vector", (lambda e, k=k: e.tensor_mul(
                        xnt[:, k * TTL:(k + 1) * TTL], ev1[:, 0:TTL], psD[:, 0:TTL])),
                        reads=["ev1a", "psD"], writes=[("xn1", k)])
                p.op("gpsimd", (lambda e, b=b, t=t: e.dma_start(
                    out=xn1_d.ap()[b, :, :, t * TTL:(t + 1) * TTL].rearrange(
                        "c p n -> p c n"),
                    in_=xnt[:, 0:DC * TTL].rearrange("p (c n) -> p c n", c=DC))),
                    reads=[("xn1", k) for k in range(DC)] + ["xnt"],
                    writes=[("xn1d", b, t // 2)])

        for b in range(BC):
            for t in range(TPB):
                p.op("sync", (lambda e, b=b, t=t: e.dma_start(
                    out=xnt[:].rearrange("p (c f) -> p c f", c=DC),
                    in_=xn1_d.ap()[b, :, :, t * TT:(t + 1) * TT].rearrange(
                        "c p f -> p c f"))),
                    extra=[p.last_w.get(("xn1d", b, t))], writes=["xnt"])
                for which, wofs, bofs, dsto in (("q", 0, 0, QO), ("k", 8192, DC, KO)):
                    for c in range(DC):
                        for k in range(DC):
                            p.op("tensor", (lambda e, c=c, k=k, wofs=wofs: e.matmul(
                                psA[:, 0:TT],
                                PW[:, wofs + (c * DC + k) * 128:
                                   wofs + (c * DC + k + 1) * 128],
                                xnt[:, ts(k, TT)],
                                start=(k == 0), stop=(k == DC - 1))),
                                reads=["wqt" if wofs == 0 else "wkt", "xnt"],
                                writes=["psAm"])
                        p.op("vector", (lambda e, c=c, bofs=bofs, dsto=dsto, t=t:
                             e.tensor_scalar_add(
                                 UB1[:, dsto + c * S + t * TT:dsto + c * S + t * TT + TT],
                                 psA[:, 0:TT], bias4[:, bofs + c:bofs + c + 1])),
                            reads=["psAm", "bq" if bofs == 0 else "bk"],
                            writes=[("qk", dsto, c, t)])
                for nblk in range(4):
                    n = t * 4 + nblk
                    for half in range(2):
                        for k in range(DC):
                            p.op("tensor", (lambda e, k=k, nblk=nblk, half=half: e.matmul(
                                psB[:, 0:512],
                                xnt[:, k * TT + nblk * 128:k * TT + nblk * 128 + 128],
                                PW[:, 24576 + k * 1024 + half * 512:
                                   24576 + k * 1024 + half * 512 + 512],
                                start=(k == 0), stop=False)),
                                reads=["wvr", "xnt"], writes=["psBm"])
                        p.op("tensor", (lambda e, half=half: e.matmul(
                            psB[:, 0:512], ones_colB[:, :],
                            bvrow[0:1, half * 512:half * 512 + 512],
                            start=False, stop=True)),
                            reads=["bvr", "onesB"], writes=["psBm"])
                        p.op("vector", (lambda e, n=n, half=half: e.tensor_copy(
                            UB1[:, VO + n * 1024 + half * 512:
                                VO + n * 1024 + half * 512 + 512],
                            psB[:, 0:512])),
                            reads=["psBm"], writes=[("v", n, half)])
            for n in range(NB):
                kw = 384 if 0 < n < NB - 1 else 256
                kb0 = (n - 1) if n > 0 else 0
                roff = 128 if n == 0 else 0
                for h in range(NH):
                    c = h // 2
                    pb = (h % 2) * 64
                    qsl = UB1[pb:pb + 64, QO + c * S + n * 128:QO + c * S + n * 128 + 128]
                    ksl = UB1[pb:pb + 64, KO + c * S + kb0 * 128:KO + c * S + kb0 * 128 + kw]
                    p.op("tensor", (lambda e, qsl=qsl, ksl=ksl, kw=kw: e.matmul(
                        psA[:, 0:kw], qsl, ksl, start=True, stop=True)),
                        reads=[("qk", QO, c, None), ("qk", KO, c, None), "qkall"],
                        writes=["psAm"])
                    if kw < 384:
                        poff = 0 if n == 0 else 256
                        p.op("vector", (lambda e, poff=poff: e.tensor_copy(
                            ev1[:, poff:poff + 128], mask[:, poff:poff + 128])),
                            reads=["mask"], writes=["ev1a"])
                    p.op("vector", (lambda e, kw=kw, roff=roff: e.tensor_add(
                        ev1[:, roff:roff + kw], psA[:, 0:kw],
                        mask[:, roff:roff + kw])),
                        reads=["psAm", "mask", "ev1a"], writes=["ev1a"])
                    p.op("vector", (lambda e: e.tensor_reduce(
                        stat[:, 0:1], ev1[:, 0:384], AX.X, ALU.max)),
                        reads=["ev1a"], writes=["statm"])
                    p.op("scalar", lambda e: e.activation(
                        stat[:, 1:2], stat[:, 0:1], AF.Copy, scale=-0.125),
                        reads=["statm"], writes=["statn"])
                    p.op("scalar", (lambda e: e.activation(
                        ev1[:, 0:384], ev1[:, 0:384], AF.Exp,
                        bias=stat[:, 1:2], scale=0.125)),
                        reads=["ev1a", "statn"], writes=["ev1a"])
                    p.op("vector", (lambda e: e.tensor_reduce(
                        stat[:, 2:3], ev1[:, 0:384], AX.X, ALU.add)),
                        reads=["ev1a"], writes=["stats"])
                    p.op("vector", lambda e: e.reciprocal(
                        stat[:, 3:4], stat[:, 2:3]),
                        reads=["stats"], writes=["statr"])
                    p.op("vector", (lambda e: e.tensor_scalar_mul(
                        probs[:, 0:384], ev1[:, 0:384], stat[:, 3:4])),
                        reads=["ev1a", "statr"], writes=["probs"])
                    nt = kw // 128
                    for j in range(nt):
                        p.op("tensor", (lambda e, j=j, roff=roff: e.transpose(
                            psT[:, j * 128:(j + 1) * 128],
                            probs[:, roff + j * 128:roff + (j + 1) * 128],
                            ident[:])),
                            reads=["probs", "ident"], writes=[("psT", j)])
                        p.op("vector", (lambda e, j=j: e.tensor_copy(
                            pT[:, j * 128:(j + 1) * 128],
                            psT[:, j * 128:(j + 1) * 128])),
                            reads=[("psT", j)], writes=[("pT", j)])
                    for j in range(nt):
                        p.op("tensor", (lambda e, j=j, kb0=kb0, h=h, nt=nt: e.matmul(
                            psB[:, 0:DH],
                            pT[:, j * 128:(j + 1) * 128],
                            UB1[:, VO + (kb0 + j) * 1024 + h * DH:
                                VO + (kb0 + j) * 1024 + h * DH + DH],
                            start=(j == 0), stop=(j == nt - 1))),
                            reads=[("pT", j), ("v", kb0 + j, 0), ("v", kb0 + j, 1)],
                            writes=["psBm"])
                    p.op("vector", (lambda e, h=h: e.tensor_copy(
                        att[:, h * DH:(h + 1) * DH], psB[:, 0:DH])),
                        reads=["psBm"], writes=[("att", h)])
                for c in range(DC):
                    p.op("tensor", (lambda e, c=c: e.transpose(
                        psT[:, 0:128], att[:, c * 128:(c + 1) * 128], ident[:])),
                        reads=[("att", 2 * c), ("att", 2 * c + 1), "ident"],
                        writes=[("psT", 0)])
                    p.op("vector", (lambda e, c=c: e.tensor_copy(
                        attf[:, c * 128:(c + 1) * 128], psT[:, 0:128])),
                        reads=[("psT", 0)], writes=[("attf", c)])
                for c in range(DC):
                    for k in range(DC):
                        p.op("tensor", (lambda e, c=c, k=k: e.matmul(
                            psA[:, 0:128],
                            PW[:, 16384 + (c * DC + k) * 128:
                               16384 + (c * DC + k + 1) * 128],
                            attf[:, k * 128:(k + 1) * 128],
                            start=(k == 0), stop=(k == DC - 1))),
                            reads=["wot", ("attf", k)], writes=["psAm"])
                    p.op("vector", (lambda e, c=c: e.tensor_scalar_add(
                        ev2[:, 0:128], psA[:, 0:128], bop[:, c:c + 1])),
                        reads=["psAm", "bop"], writes=["ev2a"])
                    p.op("gpsimd", (lambda e, c=c, b=b, n=n: e.dma_start(
                        out=s1_d.ap()[b, c, :, n * 128:(n + 1) * 128],
                        in_=ev2[:, 0:128])),
                        reads=["ev2a"], writes=[("s1", b, c, n)])

        for wi, (src_t, dst, dofs) in enumerate((
                (wg_i, "UB1", 0), (wu_i, "UB1", 24576), (wd_i, "PW", 0))):
            for ch in range(48):
                p.op("sync", (lambda e, src_t=src_t, ch=ch: e.dma_start(
                    out=ints[:], in_=src_t[:, ch * 512:(ch + 1) * 512])),
                    reads=["qkall", "vall", "wot", "wqt", "wkt", "wvr"] if (wi == 0 and ch == 0) else [],
                    writes=["ints"])
                buf = UB1 if dst == "UB1" else PW
                p.op("vector", (lambda e, buf=buf, dofs=dofs, ch=ch: e.tensor_scalar_mul(
                    buf[:, dofs + ch * 512:dofs + (ch + 1) * 512],
                    ints[:], 1.0 / 127.0)),
                    reads=["ints"], writes=[("wded", wi)])
        for b in range(BC):
            for t2 in range(S // TT2):
                t = t2 // 2
                o2 = (t2 % 2) * TT2
                p.op("sync", (lambda e, b=b, t2=t2: e.dma_start(
                    out=xnt[:, 0:DC * TT2].rearrange("p (c f) -> p c f", c=DC),
                    in_=xn1_d.ap()[b, :, :, t2 * TT2:(t2 + 1) * TT2].rearrange(
                        "c p f -> p c f"))),
                    extra=[p.last_w.get(("xn1d", b, t))], writes=["xnt"])
                for f in range(24):
                    for k in range(DC):
                        p.op("tensor", (lambda e, f=f, k=k: e.matmul(
                            psA[:, 0:TT2],
                            UB1[:, (f * 8 + k) * 128:(f * 8 + k + 1) * 128],
                            xnt[:, k * TT2:(k + 1) * TT2],
                            start=(k == 0), stop=(k == DC - 1))),
                            reads=[("wded", 0), "xnt"], writes=["psAm"])
                    p.op("scalar", lambda e: e.activation(
                        ev1[:, 0:TT2], psA[:, 0:TT2], AF.Silu),
                        reads=["psAm"], writes=["ev1a"])
                    for k in range(DC):
                        p.op("tensor", (lambda e, f=f, k=k: e.matmul(
                            psB[:, 0:TT2],
                            UB1[:, 24576 + (f * 8 + k) * 128:24576 + (f * 8 + k + 1) * 128],
                            xnt[:, k * TT2:(k + 1) * TT2],
                            start=(k == 0), stop=(k == DC - 1))),
                            reads=[("wded", 1), "xnt"], writes=["psBm"])
                    p.op("vector", (lambda e, f=f: e.tensor_mul(
                        hid[:, f * TT2:(f + 1) * TT2], ev1[:, 0:TT2], psB[:, 0:TT2])),
                        reads=["ev1a", "psBm"], writes=[("hid", f)])
                for c in range(DC):
                    for k in range(24):
                        p.op("tensor", (lambda e, c=c, k=k: e.matmul(
                            psA[:, 0:TT2],
                            PW[:, (c * 24 + k) * 128:(c * 24 + k + 1) * 128],
                            hid[:, k * TT2:(k + 1) * TT2],
                            start=(k == 0), stop=(k == 23))),
                            reads=[("wded", 2), ("hid", k)], writes=["psAm"])
                    p.op("scalar", lambda e: e.activation(
                        ev1[:, 0:TT2], psA[:, 0:TT2], AF.Silu),
                        reads=["psAm"], writes=["ev1a"])
                    p.op("sync", (lambda e, c=c, b=b, t2=t2: e.dma_start(
                        out=ev2[:, 0:TT2],
                        in_=s1_d.ap()[b, c, :, t2 * TT2:(t2 + 1) * TT2])),
                        extra=[p.last_w.get(("s1", b, c, t2 * 2 + j)) for j in range(2)],
                        writes=["ev2a"])
                    p.op("vector", lambda e: e.tensor_mul(
                        ev1[:, TT2:2 * TT2], ev1[:, 0:TT2], ev2[:, 0:TT2]),
                        reads=["ev1a", "ev2a"], writes=["ev1b"])
                    p.op("sync", (lambda e, c=c, b=b, t2=t2: e.dma_start(
                        out=ev2[:, 0:TT2],
                        in_=x1_i.ap()[b, c, :, t2 * TT2:(t2 + 1) * TT2])),
                        writes=["ev2a"])
                    p.op("vector", lambda e: e.tensor_add(
                        ev1[:, 0:TT2], ev1[:, TT2:2 * TT2], ev2[:, 0:TT2]),
                        reads=["ev1b", "ev2a"], writes=["ev1a"])
                    p.op("gpsimd", (lambda e, c=c, b=b, t2=t2: e.dma_start(
                        out=out_o.ap()[b, c, :, t2 * TT2:(t2 + 1) * TT2],
                        in_=ev1[:, 0:TT2])),
                        reads=["ev1a"])
        p.build()
    return nc
